# version 02
# baseline (speedup 1.0000x reference)
import numpy as np
import ml_dtypes
from contextlib import ExitStack
import concourse.bass as bass
import concourse.mybir as mybir
from concourse.bass_utils import run_bass_kernel_spmd

F32 = mybir.dt.float32
BF16 = mybir.dt.bfloat16
I32 = mybir.dt.int32
AF = mybir.ActivationFunctionType
ALU = mybir.AluOpType
AX = mybir.AxisListType.X

E = 2048
HID = 5632
S = 8192
B = 2
NH = 16
DH = 128
G = 4
WIN = 512
BLK = 1024
NCHUNK = E // 128
SCALE_A = 1.0 / float(np.sqrt(DH))
NEG = -1e30

STAGE = 4
SERIALIZE = set()


def p3_rows():
    rows = {}
    n = 0
    for sp in range(8):
        rows[("E", sp, "q")] = n; n += 1
        for i in range(4): rows[("E", sp, "K", i)] = n; n += 1
        for i in range(8): rows[("E", sp, "V", i)] = n; n += 1
        rows[("E", sp, "st")] = n; n += 1
    for sp in range(8):
        rows[("O", sp, "q")] = n; n += 1
        for i in range(8): rows[("O", sp, "K", i)] = n; n += 1
        for i in range(16): rows[("O", sp, "V", i)] = n; n += 1
        rows[("O", sp, "st")] = n; n += 1
    for u in range(24):
        rows[("F", u, "q")] = n; n += 1
        rows[("F", u, "stg")] = n; n += 1
        for i in range(8): rows[("F", u, "K", i)] = n; n += 1
        for i in range(16): rows[("F", u, "V", i)] = n; n += 1
        rows[("F", u, "st")] = n; n += 1
    assert n == 968
    return rows

P3ROWS = p3_rows()


SEMS = ["sd", "sg", "sv", "sa", "sp", "so",
        "dx0", "dx1", "dx2", "dwv", "dwq0", "dwq1", "dwo", "dmk", "dmisc",
        "dg0", "dg1", "dd0", "dd1", "dcw", "ds0", "ds1",
        "dk0", "dk1", "dv0", "dv1", "dq0", "dq1", "dst0", "dst1"]


class Em:

    def __init__(self, eng, proxy, sems, sems_cc):
        self.eng = eng
        self.p = proxy
        self.sems = sems
        self.sems_cc = sems_cc
        self.cnt = dict.fromkeys(SEMS, 0)

    sect = "top"

    def do(self, eng, incs, fn):
        ser = ("all" in SERIALIZE) or (Em.sect in SERIALIZE)
        if ser and not incs:
            incs = [({"sync": "sd", "gpsimd": "sg", "vector": "sv",
                      "scalar": "sa", "tensor": "sp"}[eng],
                     16 if eng in ("sync", "gpsimd") else 1)]
        if eng == self.eng:
            if ser:
                self.barrier()
            ins = fn(self.p)
            for s, k in incs:
                ins.then_inc(self.sems[s], k)
        tok = None
        for s, k in incs:
            self.cnt[s] += k
            tok = (s, self.cnt[s])
        return tok

    def fdo(self, eng, sem, fn):
        t = self.do(eng, [(sem, 1)], fn)
        self.wait(eng, t)
        return t

    def wait(self, eng, *toks):
        for tok in toks:
            if tok is None:
                continue
            s, v = tok
            if eng == self.eng:
                self.p.wait_ge(self.sems[s], v)

    def barrier(self):
        caps = [(s, self.cnt[s]) for s in SEMS if self.cnt[s] > 0]
        for s, v in caps:
            if self.eng is not None:
                self.p.wait_ge(self.sems[s], v)

    def cc(self, idx, ins_, outs_):
        if self.eng == "gpsimd":
            self.p.collective_compute(
                "AllGather", ALU.bypass, replica_groups=[list(range(8))],
                ins=[ins_], outs=[outs_]).then_inc(self.sems_cc[idx])

    def cc_wait(self, eng, idx):
        if eng == self.eng:
            self.p.wait_ge(self.sems_cc[idx], 1)


def build_program():
    nc = bass.Bass("TRN2", target_bir_lowering=False, debug=False)

    dram = {}
    def din(name, shape, dt):
        dram[name] = nc.dram_tensor(name, shape, dt, kind="ExternalInput").ap()
        return dram[name]
    def dout(name, shape, dt):
        dram[name] = nc.dram_tensor(name, shape, dt, kind="ExternalOutput").ap()
        return dram[name]
    dram_h = {}
    def dint(name, shape, dt):
        t = nc.dram_tensor(name, shape, dt)
        dram[name] = t.ap()
        dram_h[name] = t
        return dram[name]

    xa = din("xa", [E, 1536], F32)
    xb = din("xb", [E, 1536], F32)
    wq_a = din("wq_a", [E, E], F32)
    wk_a = din("wk_a", [E, E], F32)
    wv_a = din("wv_a", [E, E], F32)
    wout = din("wout", [E, E], F32)
    mska = din("mska", [16, 128, 640], F32)
    identbf = din("identbf", [128, 128], BF16)
    wg = din("wg", [E, HID], I32)
    wu = din("wu", [E, HID], I32)
    wd = din("wd", [HID, E], I32)
    sg_i = din("sg", [128, 44], F32)
    su_i = din("su", [128, 44], F32)
    sd_i = din("sd", [128, 16], F32)
    wqb = din("wqb", [E, E], F32)
    wkb = din("wkb", [E, 512], F32)
    wvb = din("wvb", [E, 512], F32)
    bqb = din("bqb", [128, 16], F32)
    bkb = din("bkb", [128, 4], F32)
    bvb = din("bvb", [1, 512], F32)
    mgm = din("mgm", [128, 3072], F32)
    idxs_i = din("idxs", [128, 968], I32)
    out = dout("out", [E, 2 * BLK], F32)
    DBG = (STAGE == 1)
    if STAGE == 2:
        dbg_x1 = dout("dbg_x1", [E, 2 * BLK], F32)
        dbg_h2 = dout("dbg_h2", [128, 32768], F32)
    if STAGE == 3:
        dbg_st = dout("dbg_st", [2048, 2112], F32)
        dbg_kg = dout("dbg_kg", [512, 2048], F32)
        dbg_vg = dout("dbg_vg", [512, 512], F32)
        dbg_qs = dout("dbg_qs", [512, 2048], F32)
        dbg_kc = dout("dbg_kc", [512, 2048], F32)
        dbg_vc = dout("dbg_vc", [512, 512], F32)
    if DBG:
        dbg_h = dout("dbg_h", [128, 16 * 1536], F32)
        dbg_avf = dout("dbg_avf", [128, 16 * 1024], F32)
        dbg_q = dout("dbg_q", [128, 1024], F32)
        dbg_k = dout("dbg_k", [128, 1536], F32)
        dbg_p = dout("dbg_p", [128, 640], F32)
        dbg_pt = dout("dbg_pt", [128, 640], F32)
        dbg_v = dout("dbg_v", [128, 10240], F32)
    x1T = dint("x1T", [E, 2 * BLK], F32)
    x2T = dint("x2T", [E, 2 * BLK], F32)
    wgbf = dint("wgbf", [E, HID], BF16)
    wubf = dint("wubf", [E, HID], BF16)
    wdbf = dint("wdbf", [HID, E], BF16)
    x3T = dint("x3T", [E, 2 * BLK], F32)
    qspill = dint("qspill", [16 * 128, 16 * 128], BF16)
    kcon = dint("kcon", [512, 2048], BF16)
    vcon = dint("vcon", [2048, 512], BF16)
    kg = dint("kg", [4096, 2048], BF16)
    vg = dint("vg", [16384, 512], BF16)
    stt_d = dint("stt_d", [16 * 128, 2112], F32)
    y3t = dint("y3t", [16 * 128, 2048], BF16)

    es = ExitStack()
    sb = lambda n, shp, dt: es.enter_context(nc.sbuf_tensor(n, shp, dt))
    ps_ = lambda n, shp, dt: es.enter_context(nc.psum_tensor(n, shp, dt))

    with es:
        AR = sb("AR", [128, 86528], BF16)
        FA = sb("FA", [128, 6400], F32)
        def bview(off, n):
            return AR[:, off:off + n]
        def fview(off, n):
            return FA[:, off:off + n]
        hslab = bview(0, 24576)
        vslab = bview(24576, 24576)
        avf = bview(49152, 16384)
        wst = bview(65536, 8192)
        maskb = bview(73728, 5120)
        wos = bview(73728, 2048)
        psb = bview(78848, 1280)
        ptsb = bview(80128, 1280)
        qsb = bview(81408, 2048)
        ksb = bview(83456, 3072)
        xsq = bview(78848, 1024)
        xbf = xsq
        h2slab = bview(0, 32768)
        zslab = bview(32768, 22528)
        wgu = bview(55296, 8192)
        wds = bview(63488, 11264)
        gsil = bview(74752, 2048)
        xbf2 = bview(76800, 1024)
        xsq2 = bview(77824, 1024)
        KS = [bview(32768, 8192), bview(40960, 8192)]
        VS = [bview(49152, 8192), bview(57344, 8192)]
        QS = [bview(65536, 2048), bview(67584, 2048)]
        PS3 = [bview(69632, 2048), bview(71680, 2048)]
        PT3 = [bview(73728, 2048), bview(75776, 2048)]
        mgmb = bview(77824, 3072)
        y3b = [bview(80896, 2048), bview(82944, 2048)]
        xt = fview(0, 1024)
        x1c = fview(1024, 1024)
        scl = fview(2048, 128)
        STS = [fview(2176, 2112), fview(4288, 2112)]
        hrows = FA[0:1, 4288:6336]
        hrowb = AR[0:1, 80128:81664]
        cols = sb("cols", [128, 64], F32)
        bqcol = cols[:, 16:32]
        bkcol = cols[:, 32:36]
        onescol = sb("onescol", [128, 1], BF16)
        onesrow = sb("onesrow", [1, 128], BF16)
        idsb = sb("idsb", [128, 128], BF16)
        idxsb = sb("idxsb", [128, 968], I32)

        psA = ps_("psA", [128, 1024], F32)
        psB = ps_("psB", [128, 1024], F32)
        pt = ps_("pt", [128, 2048], BF16)
        av = ps_("av", [128, 512], F32)
        m1 = ps_("m1", [128, 512], F32)

        sems = {s_: es.enter_context(nc.semaphore(s_)) for s_ in SEMS}
        sems_cc = [es.enter_context(nc.semaphore(f"cc{i}")) for i in range(2)]

        def program(EM: Em):
            T = {}

            t_id = EM.do("sync", [("dmisc", 16)], lambda p: p.dma_start(out=idsb[:, :], in_=identbf))
            EM.do("vector", [], lambda p: p.memset(onescol[:, :], 1.0))
            t_ones = EM.do("vector", [("sv", 1)], lambda p: p.memset(onesrow[:, :], 1.0))
            EM.wait("tensor", t_id, t_ones)

            def ln_to_h(src_ap, ncols, dst_stride, dst=None, bx=None, bq=None):
                if dst is None: dst = hslab
                if bq is None: bq = xsq
                ngr = ncols // 512
                t_h = None
                sect0 = Em.sect
                for tg in range(ngr):
                    Em.sect = sect0 + "_stats"
                    t_mm2 = None
                    for dc in range(NCHUNK):
                        sl = dc % 2
                        bsl = dc % 2
                        dreg = dc * dst_stride + tg * 512
                        EM.wait("sync", T.get(("xt", sl)))
                        t_x = EM.do("sync", [(f"dx{sl}", 16)], lambda p, dc=dc, sl=sl, tg=tg: p.dma_start(
                            out=xt[:, sl * 512:(sl + 1) * 512],
                            in_=src_ap[dc * 128:(dc + 1) * 128, tg * 512:(tg + 1) * 512]))
                        EM.wait("vector", t_x, T.get(("lncast", tg % 2, dc)))
                        t_bf = EM.do("vector", [("sv", 1)], lambda p, sl=sl, dreg=dreg: p.tensor_copy(
                            out=dst[:, dreg:dreg + 512], in_=xt[:, sl * 512:(sl + 1) * 512]))
                        T[("xt", sl)] = t_bf
                        EM.wait("scalar", t_bf, T.get(("xsq", bsl)))
                        t_sq = EM.do("scalar", [("sa", 1)], lambda p, bsl=bsl, dreg=dreg: p.activation(
                            out=bq[:, bsl * 512:(bsl + 1) * 512], in_=dst[:, dreg:dreg + 512],
                            func=AF.Square))
                        if dc == 0:
                            EM.wait("tensor", T.get("lnpsguard"))
                        EM.wait("tensor", t_bf, t_sq)
                        EM.do("tensor", [], lambda p, dc=dc, dreg=dreg: p.matmul(
                            out=m1[0:1, 0:512], lhsT=onescol[:, :],
                            rhs=dst[:, dreg:dreg + 512],
                            start=(dc == 0), stop=(dc == NCHUNK - 1)))
                        t_mm2 = EM.do("tensor", [("sp", 1)], lambda p, dc=dc, bsl=bsl: p.matmul(
                            out=av[0:1, 0:512], lhsT=onescol[:, :],
                            rhs=bq[:, bsl * 512:(bsl + 1) * 512],
                            start=(dc == 0), stop=(dc == NCHUNK - 1)))
                        T[("xsq", bsl)] = t_mm2
                    Em.sect = sect0 + "_rows"
                    R = lambda i: hrows[0:1, i * 512:(i + 1) * 512]
                    EM.wait("vector", t_mm2)
                    EM.fdo("vector", "sv", lambda p: p.tensor_scalar_mul(
                        out=R(0), in0=m1[0:1, 0:512], scalar1=1.0 / E))
                    t_r = EM.fdo("vector", "sv", lambda p: p.tensor_scalar_mul(
                        out=R(1), in0=av[0:1, 0:512], scalar1=1.0 / E))
                    EM.wait("scalar", t_r)
                    t_m2 = EM.do("scalar", [("sa", 1)], lambda p: p.activation(
                        out=R(2), in_=R(0), func=AF.Square))
                    EM.wait("vector", t_m2)
                    EM.fdo("vector", "sv", lambda p: p.tensor_sub(
                        out=R(3), in0=R(1), in1=R(2)))
                    t_var = EM.fdo("vector", "sv", lambda p: p.tensor_scalar_add(
                        out=R(3), in0=R(3), scalar1=1e-5))
                    EM.wait("scalar", t_var)
                    t_sq2 = EM.do("scalar", [("sa", 1)], lambda p: p.activation(
                        out=R(1), in_=R(3), func=AF.Sqrt))
                    EM.wait("vector", t_sq2)
                    EM.fdo("vector", "sv", lambda p: p.reciprocal(out=R(2), in_=R(1)))
                    EM.fdo("vector", "sv", lambda p: p.tensor_mul(out=R(3), in0=R(0), in1=R(2)))
                    EM.fdo("vector", "sv", lambda p: p.tensor_scalar_mul(
                        out=R(1), in0=R(3), scalar1=-1.0))
                    EM.fdo("vector", "sv", lambda p: p.tensor_copy(
                        out=hrowb[0:1, 0:512], in_=R(2)))
                    EM.fdo("vector", "sv", lambda p: p.tensor_copy(
                        out=R(3), in_=hrowb[0:1, 0:512]))
                    EM.fdo("vector", "sv", lambda p: p.tensor_sub(
                        out=R(0), in0=R(2), in1=R(3)))
                    EM.fdo("vector", "sv", lambda p: p.tensor_copy(
                        out=hrowb[0:1, 512:1024], in_=R(0)))
                    t_rows = EM.fdo("vector", "sv", lambda p: p.tensor_copy(
                        out=hrowb[0:1, 1024:1536], in_=R(1)))
                    EM.wait("tensor", t_rows, T.get("lnbcguard"))
                    EM.do("tensor", [], lambda p: p.matmul(
                        out=psA[:, 0:512], lhsT=onesrow[0:1, :], rhs=hrowb[0:1, 0:512],
                        start=True, stop=False))
                    EM.do("tensor", [], lambda p: p.matmul(
                        out=psA[:, 0:512], lhsT=onesrow[0:1, :], rhs=hrowb[0:1, 512:1024],
                        start=False, stop=True))
                    t_bc = EM.do("tensor", [("sp", 1)], lambda p: p.matmul(
                        out=psB[:, 0:512], lhsT=onesrow[0:1, :], rhs=hrowb[0:1, 1024:1536],
                        start=True, stop=True))
                    Em.sect = sect0 + "_norm"
                    for dc in range(NCHUNK):
                        dreg = dc * dst_stride + tg * 512
                        xs = dc % 2
                        EM.wait("vector", t_bc, T.get(("x1c", xs)))
                        EM.fdo("vector", "sv", lambda p, dreg=dreg, xs=xs: p.tensor_mul(
                            out=x1c[:, xs * 512:(xs + 1) * 512],
                            in0=dst[:, dreg:dreg + 512], in1=psA[:, 0:512]))
                        t_h = EM.fdo("vector", "sv", lambda p, dreg=dreg, xs=xs: p.tensor_add(
                            out=dst[:, dreg:dreg + 512],
                            in0=x1c[:, xs * 512:(xs + 1) * 512], in1=psB[:, 0:512]))
                        T[("x1c", xs)] = t_h
                        T[("lncast", tg % 2, dc)] = None
                        T[("lncast", (tg + 1) % 2, dc)] = t_h
                    T["lnpsguard"] = t_h
                    T["lnbcguard"] = t_h
                Em.sect = sect0
                return t_h

            def attn_a_slot(slot, src_ap):
                t_ln = ln_to_h(src_ap, 1536, 1536)
                t_vdone = None
                for nb in range(4):
                    bsl = 0
                    EM.wait("gpsimd", T.get(("wstv", bsl)))
                    t_w = EM.do("gpsimd", [("dwv", 16)], lambda p, nb=nb: p.dma_start(
                        out=wst[:, 0:8192].rearrange("p (c n) -> p c n", n=512),
                        in_=wv_a[:, nb * 512:(nb + 1) * 512].rearrange("(c p) n -> p c n", p=128)))
                    EM.wait("tensor", t_w, t_ln)
                    for tt in range(12):
                        bank = tt % 2
                        pdst = psA if bank == 0 else psB
                        EM.wait("tensor", T.get(("vev", bank)))
                        t_mm = None
                        for dc in range(NCHUNK):
                            t_mm = EM.do("tensor", [("sp", 1)] if dc == NCHUNK - 1 else [],
                                         lambda p, tt=tt, dc=dc, bsl=bsl, pdst=pdst: p.matmul(
                                out=pdst[:, 0:512],
                                lhsT=hslab[:, dc * 1536 + tt * 128: dc * 1536 + tt * 128 + 128],
                                rhs=wst[:, dc * 512: dc * 512 + 512],
                                start=(dc == 0), stop=(dc == NCHUNK - 1)))
                        EM.wait("scalar", t_mm)
                        t_ev = EM.do("scalar", [("sa", 1)], lambda p, tt=tt, nb=nb, pdst=pdst: p.activation(
                            out=vslab[:, tt * 2048 + nb * 512: tt * 2048 + nb * 512 + 512],
                            in_=pdst[:, 0:512], func=AF.Copy))
                        T[("vev", bank)] = t_ev
                        t_vdone = t_ev
                    T[("wstv", bsl)] = t_vdone
                T[("wstqk", 0)] = t_vdone
                T[("wstqk", 1)] = t_vdone

                t_msk = EM.do("gpsimd", [("dmk", 16)], lambda p, slot=slot: p.dma_start(
                    out=maskb[:, :].rearrange("p (q n) -> p q n", n=640),
                    in_=mska[slot * 8:(slot + 1) * 8].rearrange("q r n -> r q n")))

                for h in range(NH):
                    hb = h % 2
                    EM.wait("gpsimd", T.get(("wstqk", hb)))
                    t_wq = EM.do("gpsimd", [(f"dwq{hb}", 16)], lambda p, h=h, hb=hb: p.dma_start(
                        out=wst[:, hb * 4096:hb * 4096 + 2048].rearrange("p (c n) -> p c n", n=128),
                        in_=wq_a[:, h * 128:(h + 1) * 128].rearrange("(c p) n -> p c n", p=128)))
                    t_wk = EM.do("gpsimd", [(f"dwq{hb}", 16)], lambda p, h=h, hb=hb: p.dma_start(
                        out=wst[:, hb * 4096 + 2048:hb * 4096 + 4096].rearrange("p (c n) -> p c n", n=128),
                        in_=wk_a[:, h * 128:(h + 1) * 128].rearrange("(c p) n -> p c n", p=128)))
                    EM.wait("tensor", t_wq, t_wk, T.get(("qk", hb)), t_vdone,
                            T.get(("scps", 0)), T.get(("scps", 1)))
                    t_qe = None
                    for tg in range(2):
                        pdst = psA if tg == 0 else psB
                        t_mm = None
                        for dc in range(NCHUNK):
                            t_mm = EM.do("tensor", [("sp", 1)] if dc == NCHUNK - 1 else [],
                                         lambda p, tg=tg, dc=dc, hb=hb, pdst=pdst: p.matmul(
                                out=pdst[:, 0:512],
                                lhsT=wst[:, hb * 4096 + dc * 128: hb * 4096 + dc * 128 + 128],
                                rhs=hslab[:, dc * 1536 + 512 + tg * 512: dc * 1536 + 512 + tg * 512 + 512],
                                start=(dc == 0), stop=(dc == NCHUNK - 1)))
                        EM.wait("scalar", t_mm)
                        t_qe = EM.do("scalar", [("sa", 1)], lambda p, tg=tg, hb=hb, pdst=pdst: p.activation(
                            out=qsb[:, hb * 1024 + tg * 512: hb * 1024 + tg * 512 + 512],
                            in_=pdst[:, 0:512], func=AF.Copy, scale=SCALE_A))
                    t_ke = None
                    EM.wait("tensor", t_qe)
                    for tg in range(3):
                        pdst = [psA, psB, psA][tg]
                        if tg == 2:
                            EM.wait("tensor", t_ke)
                        t_mm = None
                        for dc in range(NCHUNK):
                            t_mm = EM.do("tensor", [("sp", 1)] if dc == NCHUNK - 1 else [],
                                         lambda p, tg=tg, dc=dc, hb=hb, pdst=pdst: p.matmul(
                                out=pdst[:, 0:512],
                                lhsT=wst[:, hb * 4096 + 2048 + dc * 128: hb * 4096 + 2048 + dc * 128 + 128],
                                rhs=hslab[:, dc * 1536 + tg * 512: dc * 1536 + tg * 512 + 512],
                                start=(dc == 0), stop=(dc == NCHUNK - 1)))
                        EM.wait("scalar", t_mm)
                        t_ke = EM.do("scalar", [("sa", 1)], lambda p, tg=tg, hb=hb, pdst=pdst: p.activation(
                            out=ksb[:, hb * 1536 + tg * 512: hb * 1536 + tg * 512 + 512],
                            in_=pdst[:, 0:512], func=AF.Copy))
                    T[("wstqk", hb)] = t_ke
                    t_kq_done = t_ke

                    t_s_last = None
                    for qt in range(8):
                        it = h * 8 + qt
                        par = it % 2
                        sc = psA if par == 0 else psB
                        avp = av if par == 0 else m1
                        pto = 0 if par == 0 else 1024
                        EM.wait("tensor", t_kq_done, T.get(("scps", par)), t_msk)
                        EM.do("tensor", [], lambda p, qt=qt, hb=hb, sc=sc: p.matmul(
                            out=sc[:, 0:512],
                            lhsT=qsb[:, hb * 1024 + qt * 128: hb * 1024 + qt * 128 + 128],
                            rhs=ksb[:, hb * 1536 + qt * 128: hb * 1536 + qt * 128 + 512],
                            start=True, stop=True))
                        t_s = EM.do("tensor", [("sp", 1)], lambda p, qt=qt, hb=hb, sc=sc: p.matmul(
                            out=sc[:, 512:640],
                            lhsT=qsb[:, hb * 1024 + qt * 128: hb * 1024 + qt * 128 + 128],
                            rhs=ksb[:, hb * 1536 + qt * 128 + 512: hb * 1536 + qt * 128 + 640],
                            start=True, stop=True))
                        t_s_last = t_s
                        EM.wait("vector", t_s)
                        EM.fdo("vector", "sv", lambda p, qt=qt, sc=sc: p.tensor_add(
                            out=sc[:, 0:640], in0=sc[:, 0:640], in1=maskb[:, qt * 640:(qt + 1) * 640]))
                        t_mx = EM.do("vector", [("sv", 1)], lambda p, sc=sc: p.reduce_max(
                            out=cols[:, 0:1], in_=sc[:, 0:640], axis=AX, negate=True))
                        EM.wait("scalar", t_mx, T.get(("psb", par)))
                        t_e = EM.do("scalar", [("sa", 1)], lambda p, par=par, sc=sc: p.activation(
                            out=psb[:, par * 640:(par + 1) * 640], in_=sc[:, 0:640], func=AF.Exp,
                            bias=cols[:, 0:1]))
                        T[("scps", par)] = t_e
                        EM.wait("vector", t_e)
                        EM.fdo("vector", "sv", lambda p, par=par: p.reduce_sum(
                            out=cols[:, 1:2], in_=psb[:, par * 640:(par + 1) * 640], axis=AX))
                        EM.fdo("vector", "sv", lambda p: p.reciprocal(out=cols[:, 2:3], in_=cols[:, 1:2]))
                        t_pn = EM.do("vector", [("sv", 1)], lambda p, par=par: p.tensor_scalar_mul(
                            out=psb[:, par * 640:(par + 1) * 640],
                            in0=psb[:, par * 640:(par + 1) * 640], scalar1=cols[:, 2:3]))
                        EM.wait("tensor", t_pn, T.get(("ptps", par)))
                        t_t = None
                        for i in range(5):
                            t_t = EM.do("tensor", [("sp", 1)] if i == 4 else [],
                                        lambda p, i=i, par=par, pto=pto: p.transpose(
                                out=pt[:, pto + i * 128: pto + i * 128 + 128],
                                in_=psb[:, par * 640 + i * 128: par * 640 + i * 128 + 128],
                                identity=idsb[:, :]))
                        T[("psb", par)] = t_t
                        EM.wait("vector", t_t, T.get(("ptsb", par)))
                        t_pte = EM.do("vector", [("sv", 1)], lambda p, par=par, pto=pto: p.tensor_copy(
                            out=ptsb[:, par * 640:(par + 1) * 640], in_=pt[:, pto: pto + 640]))
                        T[("ptps", par)] = t_pte
                        EM.wait("tensor", t_pte, T.get(("avps", par)))
                        t_av = None
                        for i in range(5):
                            t_av = EM.do("tensor", [("sp", 1)] if i == 4 else [],
                                         lambda p, i=i, qt=qt, h=h, par=par, avp=avp: p.matmul(
                                out=avp[:, 0:128],
                                lhsT=vslab[:, (qt + i) * 2048 + h * 128: (qt + i) * 2048 + h * 128 + 128],
                                rhs=ptsb[:, par * 640 + i * 128: par * 640 + i * 128 + 128],
                                start=(i == 0), stop=(i == 4)))
                        T[("ptsb", par)] = t_av
                        EM.wait("scalar", t_av)
                        t_avf = EM.do("scalar", [("sa", 1)], lambda p, h=h, qt=qt, avp=avp: p.activation(
                            out=avf[:, h * 1024 + qt * 128: h * 1024 + qt * 128 + 128],
                            in_=avp[:, 0:128], func=AF.Copy))
                        T[("avps", par)] = t_avf
                    T[("qk", hb)] = t_s_last
                t_attn_done = T[("avps", (NH * 8 - 1) % 2)]

                for ec in range(NCHUNK):
                    eb = 0
                    EM.wait("gpsimd", T.get(("wos", eb)), t_attn_done if ec == 0 else None)
                    t_wo = EM.do("gpsimd", [("dwo", 16)], lambda p, ec=ec: p.dma_start(
                        out=wos[:, 0:2048].rearrange("p (c n) -> p c n", n=128),
                        in_=wout[:, ec * 128:(ec + 1) * 128].rearrange("(c p) n -> p c n", p=128)))
                    t_mm_last = None
                    for tg in range(2):
                        it = ec * 2 + tg
                        par = it % 2
                        pdst = psA if par == 0 else psB
                        EM.wait("tensor", t_wo, t_attn_done, T.get(("ops", par)),
                                T.get(("scps", 0)), T.get(("scps", 1)))
                        t_mm = None
                        for dc in range(NCHUNK):
                            t_mm = EM.do("tensor", [("sp", 1)] if dc == NCHUNK - 1 else [],
                                         lambda p, tg=tg, dc=dc, pdst=pdst: p.matmul(
                                out=pdst[:, 0:512],
                                lhsT=wos[:, dc * 128: dc * 128 + 128],
                                rhs=avf[:, dc * 1024 + tg * 512: dc * 1024 + tg * 512 + 512],
                                start=(dc == 0), stop=(dc == NCHUNK - 1)))
                        t_mm_last = t_mm
                        xs = par
                        EM.wait("sync", T.get(("xt", xs)))
                        t_x0 = EM.do("sync", [(f"dx{xs}", 16)], lambda p, ec=ec, tg=tg, xs=xs: p.dma_start(
                            out=xt[:, xs * 512:(xs + 1) * 512],
                            in_=src_ap[ec * 128:(ec + 1) * 128, 512 + tg * 512: 512 + tg * 512 + 512]))
                        EM.wait("vector", t_mm, t_x0, T.get(("x1c", xs)))
                        t_add = EM.do("vector", [("sv", 1)], lambda p, xs=xs, pdst=pdst: p.tensor_add(
                            out=x1c[:, xs * 512:(xs + 1) * 512],
                            in0=pdst[:, 0:512], in1=xt[:, xs * 512:(xs + 1) * 512]))
                        T[("ops", par)] = t_add
                        T[("xt", xs)] = t_add
                        EM.wait("sync", t_add)
                        t_w1 = EM.do("sync", [(f"ds{xs}", 16)], lambda p, ec=ec, tg=tg, xs=xs, slot=slot: p.dma_start(
                            out=x1T[ec * 128:(ec + 1) * 128,
                                    slot * 1024 + tg * 512: slot * 1024 + tg * 512 + 512],
                            in_=x1c[:, xs * 512:(xs + 1) * 512]))
                        if STAGE == 1:
                            t_w1 = EM.do("sync", [(f"ds{xs}", 16)], lambda p, ec=ec, tg=tg, xs=xs, slot=slot: p.dma_start(
                                out=out[ec * 128:(ec + 1) * 128,
                                        slot * 1024 + tg * 512: slot * 1024 + tg * 512 + 512],
                                in_=x1c[:, xs * 512:(xs + 1) * 512]))
                        T[("x1c", xs)] = t_w1
                    T[("wos", eb)] = t_mm_last

            def ffn_block(xsrcT, xdstT, first_pass, write_out):
                Em.sect = "ln2"
                t_ln = ln_to_h(xsrcT, 2048, 2048, dst=h2slab, bx=xbf2, bq=xsq2)
                if first_pass:
                    t_sc = EM.do("sync", [("dmisc", 16)], lambda p: p.dma_start(
                        out=scl[:, 0:44], in_=sg_i))
                    t_sc = EM.do("sync", [("dmisc", 16)], lambda p: p.dma_start(
                        out=scl[:, 44:88], in_=sg_i)) if False else EM.do("sync", [("dmisc", 16)], lambda p: p.dma_start(
                        out=scl[:, 44:88], in_=su_i))
                    t_sc = EM.do("sync", [("dmisc", 16)], lambda p: p.dma_start(
                        out=scl[:, 88:104], in_=sd_i))
                    T["sclt"] = t_sc
                EM.wait("scalar", T.get("sclt"))
                EM.wait("vector", T.get("sclt"))
                for tg in range(4):
                    Em.sect = "gateup"
                    for hc in range(44):
                        par = hc % 2
                        if True:
                            if first_pass and tg == 0:
                                EM.wait("gpsimd", T.get(("wgur", par)), T.get(("wgucw", par)))
                                t_lg = EM.do("gpsimd", [(f"dg{par}", 16)], lambda p, hc=hc, par=par: p.dma_start(
                                    out=wgu[:, par * 2048:par * 2048 + 2048].rearrange("p (c n) -> p c n", n=128),
                                    in_=wg[:, hc * 128:(hc + 1) * 128].rearrange("(c p) n -> p c n", p=128)))
                                t_lu = EM.do("gpsimd", [(f"dg{par}", 16)], lambda p, hc=hc, par=par: p.dma_start(
                                    out=wgu[:, 4096 + par * 2048: 4096 + par * 2048 + 2048].rearrange("p (c n) -> p c n", n=128),
                                    in_=wu[:, hc * 128:(hc + 1) * 128].rearrange("(c p) n -> p c n", p=128)))
                                EM.wait("vector", t_lg, t_lu)
                                EM.fdo("vector", "sv", lambda p, par=par: p.tensor_scalar_add(
                                    out=wgu[:, par * 2048:par * 2048 + 2048],
                                    in0=wgu[:, par * 2048:par * 2048 + 2048], scalar1=-128.0))
                                t_cv = EM.fdo("vector", "sv", lambda p, par=par: p.tensor_scalar_add(
                                    out=wgu[:, 4096 + par * 2048: 4096 + par * 2048 + 2048],
                                    in0=wgu[:, 4096 + par * 2048: 4096 + par * 2048 + 2048], scalar1=-128.0))
                                EM.wait("sync", t_cv)
                                EM.do("sync", [("dcw", 16)], lambda p, hc=hc, par=par: p.dma_start(
                                    out=wgbf[:, hc * 128:(hc + 1) * 128].rearrange("(c p) n -> p c n", p=128),
                                    in_=wgu[:, par * 2048:par * 2048 + 2048].rearrange("p (c n) -> p c n", n=128)))
                                t_cw = EM.do("sync", [("dcw", 16)], lambda p, hc=hc, par=par: p.dma_start(
                                    out=wubf[:, hc * 128:(hc + 1) * 128].rearrange("(c p) n -> p c n", p=128),
                                    in_=wgu[:, 4096 + par * 2048: 4096 + par * 2048 + 2048].rearrange("p (c n) -> p c n", n=128)))
                                t_w = t_cv
                                T[("wgucw", par)] = t_cw
                                T["lastcw"] = t_cw
                            else:
                                EM.wait("sync", T.get(("wgur", par)), T.get(("wgucw", par)), T.get("lastcw"))
                                EM.do("sync", [(f"dg{par}", 16)], lambda p, hc=hc, par=par: p.dma_start(
                                    out=wgu[:, par * 2048:par * 2048 + 2048].rearrange("p (c n) -> p c n", n=128),
                                    in_=wgbf[:, hc * 128:(hc + 1) * 128].rearrange("(c p) n -> p c n", p=128)))
                                t_w = EM.do("sync", [(f"dg{par}", 16)], lambda p, hc=hc, par=par: p.dma_start(
                                    out=wgu[:, 4096 + par * 2048: 4096 + par * 2048 + 2048].rearrange("p (c n) -> p c n", n=128),
                                    in_=wubf[:, hc * 128:(hc + 1) * 128].rearrange("(c p) n -> p c n", p=128)))
                            T[("wgut", par)] = t_w
                        EM.wait("tensor", T.get(("wgut", par)),
                                t_ln, T.get(("gps", par)), T.get(("ups", par)),
                                T.get(("dps", par)))
                        t_g = None
                        for dc in range(NCHUNK):
                            t_g = EM.do("tensor", [("sp", 1)] if dc == NCHUNK - 1 else [],
                                        lambda p, dc=dc, par=par, tg=tg: p.matmul(
                                out=psA[:, par * 512:(par + 1) * 512],
                                lhsT=wgu[:, par * 2048 + dc * 128: par * 2048 + dc * 128 + 128],
                                rhs=h2slab[:, dc * 2048 + tg * 512: dc * 2048 + tg * 512 + 512],
                                start=(dc == 0), stop=(dc == NCHUNK - 1)))
                        t_u = None
                        for dc in range(NCHUNK):
                            t_u = EM.do("tensor", [("sp", 1)] if dc == NCHUNK - 1 else [],
                                        lambda p, dc=dc, par=par, tg=tg: p.matmul(
                                out=psB[:, par * 512:(par + 1) * 512],
                                lhsT=wgu[:, 4096 + par * 2048 + dc * 128: 4096 + par * 2048 + dc * 128 + 128],
                                rhs=h2slab[:, dc * 2048 + tg * 512: dc * 2048 + tg * 512 + 512],
                                start=(dc == 0), stop=(dc == NCHUNK - 1)))
                        T[("wgur", par)] = t_u
                        EM.wait("scalar", t_g, T.get(("gsil", par)))
                        t_si = EM.do("scalar", [("sa", 1)], lambda p, hc=hc, par=par: p.activation(
                            out=gsil[:, par * 512:(par + 1) * 512],
                            in_=psA[:, par * 512:(par + 1) * 512], func=AF.Silu,
                            scale=scl[:, hc:hc + 1]))
                        T[("gps", par)] = t_si
                        EM.wait("vector", t_u, t_si, T.get("zguard"))
                        t_z = EM.do("vector", [("sv", 1)], lambda p, hc=hc, par=par: p.scalar_tensor_tensor(
                            out=zslab[:, hc * 512:(hc + 1) * 512],
                            in0=psB[:, par * 512:(par + 1) * 512], scalar=scl[:, 44 + hc:44 + hc + 1],
                            in1=gsil[:, par * 512:(par + 1) * 512],
                            op0=ALU.mult, op1=ALU.mult))
                        T[("ups", par)] = t_z
                        T[("gsil", par)] = t_z
                        T[("zw", hc)] = t_z
                    t_zlast = T[("zw", 43)]
                    Em.sect = "down"
                    for ec in range(NCHUNK):
                        par = ec % 2
                        if True:
                            if first_pass and tg == 0:
                                EM.wait("gpsimd", T.get(("wdsr", par)), T.get(("wdscw", par)))
                                t_ld = EM.do("gpsimd", [(f"dd{par}", 16)], lambda p, ec=ec, par=par: p.dma_start(
                                    out=wds[:, par * 5632:par * 5632 + 5632].rearrange("p (c n) -> p c n", n=128),
                                    in_=wd[:, ec * 128:(ec + 1) * 128].rearrange("(c p) n -> p c n", p=128)))
                                EM.wait("vector", t_ld)
                                t_cv = EM.fdo("vector", "sv", lambda p, par=par: p.tensor_scalar_add(
                                    out=wds[:, par * 5632:par * 5632 + 5632],
                                    in0=wds[:, par * 5632:par * 5632 + 5632], scalar1=-128.0))
                                EM.wait("sync", t_cv)
                                t_cw = EM.do("sync", [("dcw", 16)], lambda p, ec=ec, par=par: p.dma_start(
                                    out=wdbf[:, ec * 128:(ec + 1) * 128].rearrange("(c p) n -> p c n", p=128),
                                    in_=wds[:, par * 5632:par * 5632 + 5632].rearrange("p (c n) -> p c n", n=128)))
                                t_wd = t_cv
                                T[("wdscw", par)] = t_cw
                                T["lastcw"] = t_cw
                            else:
                                EM.wait("sync", T.get(("wdsr", par)), T.get(("wdscw", par)), T.get("lastcw"))
                                t_wd = EM.do("sync", [(f"dd{par}", 16)], lambda p, ec=ec, par=par: p.dma_start(
                                    out=wds[:, par * 5632:par * 5632 + 5632].rearrange("p (c n) -> p c n", n=128),
                                    in_=wdbf[:, ec * 128:(ec + 1) * 128].rearrange("(c p) n -> p c n", p=128)))
                            T[("wdst", par)] = t_wd
                        EM.wait("tensor", T.get(("wdst", par)), t_zlast,
                                T.get(("dps", par)), T.get(("gps", par)), T.get(("ups", par)))
                        pd = psA if par == 0 else psB
                        po = 0 if par == 0 else 512
                        t_dn = None
                        for hc in range(44):
                            t_dn = EM.do("tensor", [("sp", 1)] if hc == 43 else [],
                                         lambda p, hc=hc, par=par, pd=pd, po=po: p.matmul(
                                out=pd[:, po:po + 512],
                                lhsT=wds[:, par * 5632 + hc * 128: par * 5632 + hc * 128 + 128],
                                rhs=zslab[:, hc * 512:(hc + 1) * 512],
                                start=(hc == 0), stop=(hc == 43)))
                        T[("wdsr", par)] = t_dn
                        xs = par
                        EM.wait("sync", T.get(("xt", xs)))
                        t_x1 = EM.do("sync", [(f"dx{xs}", 16)], lambda p, ec=ec, tg=tg, xs=xs: p.dma_start(
                            out=xt[:, xs * 512:(xs + 1) * 512],
                            in_=xsrcT[ec * 128:(ec + 1) * 128, tg * 512:(tg + 1) * 512]))
                        EM.wait("vector", t_dn, t_x1, T.get(("x1c", xs)))
                        t_add = EM.do("vector", [("sv", 1)], lambda p, ec=ec, xs=xs, pd=pd, po=po: p.scalar_tensor_tensor(
                            out=x1c[:, xs * 512:(xs + 1) * 512],
                            in0=pd[:, po:po + 512], scalar=scl[:, 88 + ec:88 + ec + 1],
                            in1=xt[:, xs * 512:(xs + 1) * 512],
                            op0=ALU.mult, op1=ALU.add))
                        T[("dps", par)] = t_add
                        T[("xt", xs)] = t_add
                        EM.wait("sync", t_add)
                        t_wx = None
                        if xdstT is not None:
                            t_wx = EM.do("sync", [(f"ds{xs}", 16)], lambda p, ec=ec, tg=tg, xs=xs: p.dma_start(
                                out=xdstT[ec * 128:(ec + 1) * 128, tg * 512:(tg + 1) * 512],
                                in_=x1c[:, xs * 512:(xs + 1) * 512]))
                        if write_out:
                            t_wx = EM.do("sync", [(f"ds{xs}", 16)], lambda p, ec=ec, tg=tg, xs=xs: p.dma_start(
                                out=out[ec * 128:(ec + 1) * 128, tg * 512:(tg + 1) * 512],
                                in_=x1c[:, xs * 512:(xs + 1) * 512]))
                        T[("x1c", xs)] = t_wx
                    T["zguard"] = T[("wdsr", 15 % 2)]
                Em.sect = "top"

            from concourse.bass import IndirectOffsetOnAxis as IOA
            kgr = kg.rearrange("q (b n) -> (q b) n", b=2)

            def qkv_b():
                for h in range(NH):
                    hb = h % 2
                    EM.wait("gpsimd", T.get(("wstqk", hb)))
                    t_wq = EM.do("gpsimd", [(f"dwq{hb}", 16)], lambda p, h=h, hb=hb: p.dma_start(
                        out=wst[:, hb * 4096:hb * 4096 + 2048].rearrange("p (c n) -> p c n", n=128),
                        in_=wqb[:, h * 128:(h + 1) * 128].rearrange("(c p) n -> p c n", p=128)))
                    t_qs = None
                    for tg in range(4):
                        par = tg % 2
                        pdst = psA if par == 0 else psB
                        EM.wait("tensor", t_wq, T.get(("qbps", par)))
                        t_mm = None
                        for dc in range(NCHUNK):
                            t_mm = EM.do("tensor", [("sp", 1)] if dc == NCHUNK - 1 else [],
                                         lambda p, tg=tg, dc=dc, hb=hb, pdst=pdst: p.matmul(
                                out=pdst[:, 0:512],
                                lhsT=wst[:, hb * 4096 + dc * 128: hb * 4096 + dc * 128 + 128],
                                rhs=h2slab[:, dc * 2048 + tg * 512: dc * 2048 + tg * 512 + 512],
                                start=(dc == 0), stop=(dc == NCHUNK - 1)))
                        EM.wait("vector", t_mm, T.get(("qev", par)))
                        t_qe = EM.do("vector", [("sv", 1)], lambda p, h=h, tg=tg, par=par, pdst=pdst: p.tensor_scalar(
                            out=y3b[0][:, par * 512:(par + 1) * 512],
                            in0=pdst[:, 0:512], scalar1=bqcol[:, h:h + 1], scalar2=SCALE_A,
                            op0=ALU.add, op1=ALU.mult))
                        T[("qbps", par)] = t_qe
                        EM.wait("sync", t_qe, T.get(("qsp", par)))
                        t_qs = EM.do("sync", [(f"ds{par}", 16)], lambda p, h=h, tg=tg, par=par: p.dma_start(
                            out=qspill.rearrange("(T p) (hh q) -> T p hh q", p=128, q=128)
                                [tg * 4:(tg + 1) * 4, :, h, :].rearrange("T p q -> p T q"),
                            in_=y3b[0][:, par * 512:(par + 1) * 512].rearrange("p (T q) -> p T q", q=128)))
                        T[("qev", par)] = t_qs
                        T[("qsp", par)] = t_qs
                    T[("wstqk", hb)] = t_qs
                for g in range(G):
                    hb = g % 2
                    EM.wait("gpsimd", T.get(("wstqk", hb)))
                    t_wk = EM.do("gpsimd", [(f"dwq{hb}", 16)], lambda p, g=g, hb=hb: p.dma_start(
                        out=wst[:, hb * 4096:hb * 4096 + 2048].rearrange("p (c n) -> p c n", n=128),
                        in_=wkb[:, g * 128:(g + 1) * 128].rearrange("(c p) n -> p c n", p=128)))
                    t_ks = None
                    for tg in range(4):
                        par = tg % 2
                        pdst = psA if par == 0 else psB
                        EM.wait("tensor", t_wk, T.get(("qbps", par)))
                        t_mm = None
                        for dc in range(NCHUNK):
                            t_mm = EM.do("tensor", [("sp", 1)] if dc == NCHUNK - 1 else [],
                                         lambda p, tg=tg, dc=dc, hb=hb, pdst=pdst: p.matmul(
                                out=pdst[:, 0:512],
                                lhsT=wst[:, hb * 4096 + dc * 128: hb * 4096 + dc * 128 + 128],
                                rhs=h2slab[:, dc * 2048 + tg * 512: dc * 2048 + tg * 512 + 512],
                                start=(dc == 0), stop=(dc == NCHUNK - 1)))
                        EM.wait("vector", t_mm, T.get(("qev", par)))
                        t_ke = EM.do("vector", [("sv", 1)], lambda p, g=g, par=par, pdst=pdst: p.tensor_scalar_add(
                            out=y3b[0][:, par * 512:(par + 1) * 512],
                            in0=pdst[:, 0:512], scalar1=bkcol[:, g:g + 1]))
                        T[("qbps", par)] = t_ke
                        EM.wait("sync", t_ke, T.get(("qsp", par)))
                        t_ks = EM.do("sync", [(f"ds{par}", 16)], lambda p, g=g, tg=tg, par=par: p.dma_start(
                            out=kcon[g * 128:(g + 1) * 128, tg * 512:(tg + 1) * 512],
                            in_=y3b[0][:, par * 512:(par + 1) * 512]))
                        T[("qev", par)] = t_ks
                        T[("qsp", par)] = t_ks
                    T[("wstqk", hb)] = t_ks
                EM.wait("gpsimd", T.get(("wstqk", 0)), T.get(("wstqk", 1)))
                t_wv = EM.do("gpsimd", [("dwv", 16)], lambda p: p.dma_start(
                    out=wst[:, 0:8192].rearrange("p (c n) -> p c n", n=512),
                    in_=wvb.rearrange("(c p) n -> p c n", p=128)))
                t_bv = EM.do("gpsimd", [("dmisc", 16)], lambda p: p.dma_start(
                    out=hrowb[0:1, 0:512], in_=bvb))
                for tt in range(16):
                    par = tt % 2
                    pdst = psA if par == 0 else psB
                    EM.wait("tensor", t_wv, t_bv, T.get(("qbps", par)))
                    for dc in range(NCHUNK):
                        EM.do("tensor", [], lambda p, tt=tt, dc=dc, pdst=pdst: p.matmul(
                            out=pdst[:, 0:512],
                            lhsT=h2slab[:, dc * 2048 + tt * 128: dc * 2048 + tt * 128 + 128],
                            rhs=wst[:, dc * 512:(dc + 1) * 512],
                            start=(dc == 0), stop=False))
                    t_mm = EM.do("tensor", [("sp", 1)], lambda p, pdst=pdst: p.matmul(
                        out=pdst[:, 0:512], lhsT=onesrow[0:1, :], rhs=hrowb[0:1, 0:512],
                        start=False, stop=True))
                    EM.wait("vector", t_mm, T.get(("qev", par)))
                    t_ve = EM.do("vector", [("sv", 1)], lambda p, par=par, pdst=pdst: p.tensor_copy(
                        out=y3b[0][:, par * 512:(par + 1) * 512], in_=pdst[:, 0:512]))
                    T[("qbps", par)] = t_ve
                    EM.wait("sync", t_ve, T.get(("qsp", par)))
                    t_vs = EM.do("sync", [(f"ds{par}", 16)], lambda p, tt=tt, par=par: p.dma_start(
                        out=vcon[tt * 128:(tt + 1) * 128, :],
                        in_=y3b[0][:, par * 512:(par + 1) * 512]))
                    T[("qev", par)] = t_vs
                    T[("qsp", par)] = t_vs

            def attn_b_unit(kind, sp, uidx, ucount):
                par = uidx % 2
                W = 1024 if kind == "E" else 2048
                nch = W // 128
                nsc = W // 512
                R_ = P3ROWS
                irow = lambda key: idxsb[:, R_[key]:R_[key] + 1]
                EM.wait("gpsimd", T.get(("qsr", par)), T.get(("stu", 0)), T.get(("stu", 1)))
                t_q = EM.do("gpsimd", [(f"dq{par}", 16)], lambda p: p.indirect_dma_start(
                    out=QS[par][:, :], out_offset=None, in_=qspill[:, :],
                    in_offset=IOA(ap=irow((kind, sp, "q")), axis=0)))
                if kind == "F":
                    t_st = EM.do("gpsimd", [(f"dst{par}", 16)], lambda p: p.indirect_dma_start(
                        out=STS[par][:, :], out_offset=None, in_=stt_d[:, :],
                        in_offset=IOA(ap=irow((kind, sp, "stg")), axis=0)))
                else:
                    t_st = None
                EM.wait("gpsimd", T.get(("ksr", par)))
                nk = 4 if kind == "E" else 8
                t_k = None
                for i in range(nk):
                    g = i if kind == "E" else i // 2
                    half = 0 if kind == "E" else i % 2
                    t_k = EM.do("gpsimd", [(f"dk{par}", 16)], lambda p, i=i, g=g, half=half: p.indirect_dma_start(
                        out=KS[par][:, g * 2048 + half * 1024: g * 2048 + half * 1024 + 1024],
                        out_offset=None, in_=kgr[:, :],
                        in_offset=IOA(ap=irow((kind, sp, "K", i)), axis=0)))
                EM.wait("gpsimd", T.get(("vsr", par)))
                t_v = None
                for i in range(nch):
                    t_v = EM.do("gpsimd", [(f"dv{par}", 16)], lambda p, i=i: p.indirect_dma_start(
                        out=VS[par][:, i * 512:(i + 1) * 512], out_offset=None, in_=vg[:, :],
                        in_offset=IOA(ap=irow((kind, sp, "V", i)), axis=0)))
                t_slast = None
                t_avlast = None
                for h in range(NH):
                    g2 = h // 4
                    hp = h % 2
                    avp = av if hp == 0 else m1
                    for ch2 in range(nsc // 2):
                        pch = psA if ch2 == 0 else psB
                        EM.wait("tensor", t_q, t_k, T.get(("sce", ch2)))
                        for i in range(2):
                            t_s = EM.do("tensor", [("sp", 1)] if i == 1 else [],
                                        lambda p, h=h, g2=g2, ch2=ch2, i=i, pch=pch, par=par: p.matmul(
                                out=pch[:, i * 512:(i + 1) * 512],
                                lhsT=QS[par][:, h * 128:(h + 1) * 128],
                                rhs=KS[par][:, g2 * 2048 + ch2 * 1024 + i * 512: g2 * 2048 + ch2 * 1024 + i * 512 + 512],
                                start=True, stop=True))
                        t_slast = t_s
                        if kind != "F":
                            st0 = (2048 - 128 * sp) if kind == "E" else (1024 - 128 * sp + 1024 * ch2)
                            EM.wait("vector", t_s)
                            EM.fdo("vector", "sv", lambda p, ch2=ch2, st0=st0, pch=pch: p.tensor_add(
                                out=pch[:, 0:1024], in0=pch[:, 0:1024],
                                in1=mgmb[:, st0:st0 + 1024]))
                            t_mx = EM.fdo("vector", "sv", lambda p, ch2=ch2, pch=pch: p.reduce_max(
                                out=cols[:, ch2:ch2 + 1], in_=pch[:, 0:1024], axis=AX, negate=True))
                            T[("mx", ch2)] = t_mx
                    if kind != "F":
                        if nsc == 4:
                            t_mn = EM.fdo("vector", "sv", lambda p: p.tensor_tensor(
                                out=cols[:, 0:1], in0=cols[:, 0:1], in1=cols[:, 1:2], op=ALU.min))
                        else:
                            t_mn = T[("mx", 0)]
                        t_mn = EM.fdo("vector", "sv", lambda p, h=h, par=par: p.tensor_copy(
                            out=STS[par][:, h * 132 + 129: h * 132 + 130], in_=cols[:, 0:1]))
                    else:
                        t_mn = t_st
                    t_e = None
                    for ch2 in range(nsc // 2):
                        pch = psA if ch2 == 0 else psB
                        EM.wait("scalar", t_mn, t_slast, T.get(("pse", par, ch2)),
                                T.get(("psm", par)))
                        t_e = EM.do("scalar", [("sa", 1)], lambda p, h=h, ch2=ch2, par=par, pch=pch: p.activation(
                            out=PS3[par][:, ch2 * 1024:(ch2 + 1) * 1024], in_=pch[:, 0:1024],
                            func=AF.Exp, bias=STS[par][:, h * 132 + 129: h * 132 + 130]))
                        T[("sce", ch2)] = t_e
                    EM.wait("vector", t_e)
                    t_sm = EM.fdo("vector", "sv", lambda p, par=par, W=W: p.reduce_sum(
                        out=cols[:, 2:3], in_=PS3[par][:, 0:W], axis=AX))
                    if kind == "F":
                        t_sm = EM.fdo("vector", "sv", lambda p, h=h, par=par: p.tensor_add(
                            out=STS[par][:, h * 132 + 128: h * 132 + 129],
                            in0=STS[par][:, h * 132 + 128: h * 132 + 129], in1=cols[:, 2:3]))
                    else:
                        t_sm = EM.fdo("vector", "sv", lambda p, h=h, par=par: p.tensor_copy(
                            out=STS[par][:, h * 132 + 128: h * 132 + 129], in_=cols[:, 2:3]))
                    T[("psm", par)] = t_sm
                    EM.wait("tensor", t_e, T.get("pte"))
                    t_t = None
                    for i in range(nch):
                        t_t = EM.do("tensor", [("sp", 1)] if i == nch - 1 else [],
                                    lambda p, i=i, par=par: p.transpose(
                            out=pt[:, i * 128:(i + 1) * 128],
                            in_=PS3[par][:, i * 128:(i + 1) * 128], identity=idsb[:, :]))
                    EM.wait("vector", t_t, T.get(("pts", par)))
                    t_pte = EM.do("vector", [("sv", 1)], lambda p, par=par, W=W: p.tensor_copy(
                        out=PT3[par][:, 0:W], in_=pt[:, 0:W]))
                    T["pte"] = t_pte
                    T[("pse", par, 0)] = t_t
                    T[("pse", par, 1)] = t_t
                    EM.wait("tensor", t_pte, t_v, T.get(("avu", hp)))
                    t_av = None
                    for i in range(nch):
                        t_av = EM.do("tensor", [("sp", 1)] if i == nch - 1 else [],
                                     lambda p, i=i, g2=g2, par=par, avp=avp: p.matmul(
                            out=avp[:, 0:128],
                            lhsT=PT3[par][:, i * 128:(i + 1) * 128],
                            rhs=VS[par][:, i * 512 + g2 * 128: i * 512 + g2 * 128 + 128],
                            start=(i == 0), stop=(i == nch - 1)))
                    t_avlast = t_av
                    T[("pts", par)] = t_av
                    EM.wait("vector", t_av, t_sm, t_mn if kind == "F" else None)
                    if kind == "F":
                        t_mg = EM.fdo("vector", "sv", lambda p, h=h, par=par, avp=avp: p.tensor_add(
                            out=STS[par][:, h * 132: h * 132 + 128],
                            in0=STS[par][:, h * 132: h * 132 + 128], in1=avp[:, 0:128]))
                    else:
                        t_mg = EM.fdo("vector", "sv", lambda p, h=h, par=par, avp=avp: p.tensor_copy(
                            out=STS[par][:, h * 132: h * 132 + 128], in_=avp[:, 0:128]))
                    T[("avu", hp)] = t_mg
                    T[("mrg", h)] = t_mg
                t_mrg_all = T[("mrg", NH - 1)]
                EM.wait("gpsimd", t_mrg_all)
                t_sc = EM.do("gpsimd", [(f"dst{par}", 16)], lambda p: p.indirect_dma_start(
                    out=stt_d[:, :], out_offset=IOA(ap=irow((kind, sp, "st")), axis=0),
                    in_=STS[par][:, :], in_offset=None))
                T[("stu", par)] = t_sc
                T[("qsr", par)] = t_slast
                T[("ksr", par)] = t_slast
                T[("vsr", par)] = t_avlast

            def p3c_finalize(t):
                par = t % 2
                EM.wait("sync", T.get(("stu", 0)), T.get(("stu", 1)), T.get(("stf", par)))
                t_st = EM.do("sync", [(f"dst{par}", 16)], lambda p, t=t: p.dma_start(
                    out=STS[par][:, :], in_=stt_d[t * 128:(t + 1) * 128, :]))
                t_y = None
                for h in range(NH):
                    EM.wait("vector", t_st, T.get(("y3w", par)) if h == 0 else None)
                    EM.fdo("vector", "sv", lambda p, h=h, par=par: p.reciprocal(
                        out=cols[:, 4:5], in_=STS[par][:, h * 132 + 128: h * 132 + 129]))
                    t_y = EM.fdo("vector", "sv", lambda p, h=h, par=par: p.tensor_scalar_mul(
                        out=y3b[par][:, h * 128:(h + 1) * 128],
                        in0=STS[par][:, h * 132: h * 132 + 128], scalar1=cols[:, 4:5]))
                T[("stf", par)] = t_y
                EM.wait("sync", t_y, T.get(("y3s", par)))
                t_ys = EM.do("sync", [(f"ds{par}", 16)], lambda p, t=t, par=par: p.dma_start(
                    out=y3t[t * 128:(t + 1) * 128, :], in_=y3b[par][:, :]))
                T[("y3w", par)] = t_ys
                T[("y3s", par)] = t_ys

            def p3d_tile(slot, qt, write_out):
                Tn = slot * 8 + qt
                tcol = slot * 1024 + qt * 128
                par = Tn % 2
                EM.wait("gpsimd", T.get(("y3r", par)))
                t_y = EM.do("gpsimd", [(f"dq{par}", 16)], lambda p, Tn=Tn, par=par: p.dma_start(
                    out=y3b[par][:, :], in_=y3t[Tn * 128:(Tn + 1) * 128, :]))
                t_tr = None
                for dc in range(NCHUNK):
                    pslot = (dc % 2) * 1024
                    EM.wait("tensor", t_y, T.get(("ptd", dc % 2)))
                    t_tr = EM.do("tensor", [("sp", 1)], lambda p, dc=dc, par=par, pslot=pslot: p.transpose(
                        out=pt[:, pslot:pslot + 128],
                        in_=y3b[par][:, dc * 128:(dc + 1) * 128], identity=idsb[:, :]))
                    sl = dc % 2
                    EM.wait("sync", T.get(("xt", sl)))
                    t_x2 = EM.do("sync", [(f"dx{sl}", 16)], lambda p, dc=dc, sl=sl, tcol=tcol: p.dma_start(
                        out=xt[:, sl * 512:sl * 512 + 128],
                        in_=x2T[dc * 128:(dc + 1) * 128, tcol:tcol + 128]))
                    xs = dc % 2
                    EM.wait("vector", t_tr, t_x2, T.get(("x1c", xs)))
                    t_add = EM.do("vector", [("sv", 1)], lambda p, dc=dc, sl=sl, xs=xs, pslot=pslot: p.tensor_add(
                        out=x1c[:, xs * 512:xs * 512 + 128],
                        in0=pt[:, pslot:pslot + 128], in1=xt[:, sl * 512:sl * 512 + 128]))
                    T[("xt", sl)] = t_add
                    T[("ptd", dc % 2)] = t_add
                    EM.wait("sync", t_add)
                    t_w = EM.do("sync", [(f"ds{xs}", 16)], lambda p, dc=dc, tcol=tcol, xs=xs: p.dma_start(
                        out=x3T[dc * 128:(dc + 1) * 128, tcol:tcol + 128],
                        in_=x1c[:, xs * 512:xs * 512 + 128]))
                    if write_out:
                        t_w = EM.do("sync", [(f"ds{xs}", 16)], lambda p, dc=dc, tcol=tcol, xs=xs: p.dma_start(
                            out=out[dc * 128:(dc + 1) * 128, tcol:tcol + 128],
                            in_=x1c[:, xs * 512:xs * 512 + 128]))
                    T[("x1c", xs)] = t_w
                T[("y3r", par)] = t_tr

            def phase3(write_out):
                ln_to_h(x2T, 2048, 2048, dst=h2slab, bx=xbf2, bq=xsq2)
                t_b = EM.do("sync", [("dmisc", 16)], lambda p: p.dma_start(out=bqcol, in_=bqb))
                t_b = EM.do("sync", [("dmisc", 16)], lambda p: p.dma_start(out=bkcol, in_=bkb))
                t_ix = EM.do("sync", [("dmisc", 16)], lambda p: p.dma_start(
                    out=idxsb[:, 0:968], in_=idxs_i[:, 0:968]))
                EM.wait("vector", t_b)
                EM.wait("gpsimd", t_ix)
                T[("qbps", 0)] = T.get("lnbcguard")
                T[("qbps", 1)] = T.get("lnbcguard")
                qkv_b()
                EM.barrier()
                if STAGE == 3:
                    for i in range(4):
                        EM.do("gpsimd", [("so", 16)], lambda p, i=i: p.dma_start(
                            out=dbg_kc[i * 128:(i + 1) * 128, :], in_=kcon[i * 128:(i + 1) * 128, :]))
                        EM.do("gpsimd", [("so", 16)], lambda p, i=i: p.dma_start(
                            out=dbg_vc[i * 128:(i + 1) * 128, :], in_=vcon[i * 128:(i + 1) * 128, :]))
                    EM.barrier()
                EM.cc(0, dram_h["kcon"].ap().opt(), dram_h["kg"].ap().opt())
                EM.cc(1, dram_h["vcon"].ap().opt(), dram_h["vg"].ap().opt())
                for e_ in ("sync", "gpsimd", "vector", "scalar", "tensor"):
                    EM.cc_wait(e_, 0)
                    EM.cc_wait(e_, 1)
                t_mg = EM.do("gpsimd", [("dmk", 16)], lambda p: p.dma_start(out=mgmb[:, :], in_=mgm))
                EM.wait("vector", t_mg)
                EM.wait("tensor", t_mg)
                uc = 0
                for sp in range(8):
                    attn_b_unit("E", sp, uc, 40); uc += 1
                for sp in range(8):
                    attn_b_unit("O", sp, uc, 40); uc += 1
                EM.barrier()
                for u in range(24):
                    attn_b_unit("F", u, uc, 40); uc += 1
                EM.barrier()
                if STAGE == 3:
                    for i in range(16):
                        EM.do("gpsimd", [("so", 16)], lambda p, i=i: p.dma_start(
                            out=dbg_st[i * 128:(i + 1) * 128, :], in_=stt_d[i * 128:(i + 1) * 128, :]))
                    for i in range(4):
                        EM.do("gpsimd", [("so", 16)], lambda p, i=i: p.dma_start(
                            out=dbg_kg[i * 128:(i + 1) * 128, :], in_=kg[i * 128:(i + 1) * 128, :]))
                        EM.do("gpsimd", [("so", 16)], lambda p, i=i: p.dma_start(
                            out=dbg_vg[i * 128:(i + 1) * 128, :], in_=vg[i * 128:(i + 1) * 128, :]))
                        EM.do("gpsimd", [("so", 16)], lambda p, i=i: p.dma_start(
                            out=dbg_qs[i * 128:(i + 1) * 128, :], in_=qspill[i * 128:(i + 1) * 128, :]))
                    EM.barrier()
                for t in range(16):
                    p3c_finalize(t)
                EM.barrier()
                for slot in range(2):
                    for qt in range(8):
                        p3d_tile(slot, qt, write_out)
                EM.barrier()

            attn_a_slot(0, xa)
            if DBG and STAGE == 1:
                EM.barrier()
                for i in range(6):
                    EM.do("gpsimd", [("so", 16)], lambda p, i=i: p.dma_start(
                        out=dbg_h[:, i * 4096:(i + 1) * 4096], in_=hslab[:, i * 4096:(i + 1) * 4096]))
                for i in range(4):
                    EM.do("gpsimd", [("so", 16)], lambda p, i=i: p.dma_start(
                        out=dbg_avf[:, i * 4096:(i + 1) * 4096], in_=avf[:, i * 4096:(i + 1) * 4096]))
                EM.do("gpsimd", [("so", 16)], lambda p: p.dma_start(out=dbg_q, in_=qsb[:, 1024:2048]))
                EM.do("gpsimd", [("so", 16)], lambda p: p.dma_start(out=dbg_k, in_=ksb[:, 1536:3072]))
                EM.do("gpsimd", [("so", 16)], lambda p: p.dma_start(out=dbg_p, in_=psb[:, 640:1280]))
                EM.do("gpsimd", [("so", 16)], lambda p: p.dma_start(out=dbg_pt, in_=ptsb[:, 640:1280]))
                for i in range(5):
                    EM.do("gpsimd", [("so", 16)], lambda p, i=i: p.dma_start(
                        out=dbg_v[:, i * 2048:(i + 1) * 2048], in_=vslab[:, (7 + i) * 2048:(8 + i) * 2048]))
            EM.barrier()
            attn_a_slot(1, xb)
            EM.barrier()
            if STAGE == 2:
                for i in range(8):
                    EM.do("gpsimd", [("so", 16)], lambda p, i=i: p.dma_start(
                        out=dbg_x1[i * 256:(i + 1) * 256, :], in_=x1T[i * 256:(i + 1) * 256, :]))
            if STAGE >= 2:
                ffn_block(x1T, x2T, first_pass=True, write_out=(STAGE == 2))
                EM.barrier()
            if STAGE >= 3:
                phase3(write_out=(STAGE == 3))
            if STAGE >= 4:
                ffn_block(x3T, None, first_pass=False, write_out=True)
                EM.barrier()
                if STAGE == 2:
                    for i in range(8):
                        EM.do("gpsimd", [("so", 16)], lambda p, i=i: p.dma_start(
                            out=dbg_h2[:, i * 4096:(i + 1) * 4096], in_=h2slab[:, i * 4096:(i + 1) * 4096]))
                    EM.barrier()

        with nc.Block() as block:
            @block.sync
            def _(p):
                program(Em("sync", p, sems, sems_cc))
            @block.gpsimd
            def _(p):
                program(Em("gpsimd", p, sems, sems_cc))
            @block.vector
            def _(p):
                program(Em("vector", p, sems, sems_cc))
            @block.scalar
            def _(p):
                program(Em("scalar", p, sems, sems_cc))
            @block.tensor
            def _(p):
                program(Em("tensor", p, sems, sems_cc))

    return nc


def _prep_core(c, I):
    x0 = I["L0_attn_a_x"]
    j0, j1 = c, 7 - c
    def xslice(b, j):
        lo = j * BLK - 512
        blkx = x0[b, max(lo, 0): j * BLK + BLK].T.astype(np.float32)
        if lo < 0:
            blkx = np.concatenate([np.zeros((E, -lo), np.float32), blkx], axis=1)
        return np.ascontiguousarray(blkx)
    wqkv = I["L0_attn_a_w_qkv"]
    m = {}
    m["xa"] = xslice(0, j0)
    m["xb"] = xslice(1, j1)
    m["wq_a"] = np.ascontiguousarray(wqkv[0:2048].T)
    m["wk_a"] = np.ascontiguousarray(wqkv[2048:4096].T)
    m["wv_a"] = np.ascontiguousarray(wqkv[4096:6144].T)
    m["wout"] = np.ascontiguousarray(I["L0_attn_a_w_out"].T)
    msk = np.zeros((16, 128, 640), np.float32)
    r = np.arange(128)[:, None]
    x = np.arange(640)[None, :]
    for sl, j in ((0, j0), (1, j1)):
        for qt in range(8):
            q0 = j * BLK + qt * 128
            allowed = (x <= 512 + r) & (x >= r) & (x >= 512 - q0)
            msk[sl * 8 + qt] = np.where(allowed, 0.0, NEG).astype(np.float32)
    m["mska"] = msk
    m["identbf"] = np.eye(128, dtype=np.float32).astype(ml_dtypes.bfloat16)
    m["wg"] = np.ascontiguousarray(I["L0_ffn_wq_gate"].T)
    m["wu"] = np.ascontiguousarray(I["L0_ffn_wq_up"].T)
    m["wd"] = np.ascontiguousarray(I["L0_ffn_wq_down"].T)
    m["sg"] = np.ascontiguousarray(I["L0_ffn_s_gate"].reshape(44, 128).T)
    m["su"] = np.ascontiguousarray(I["L0_ffn_s_up"].reshape(44, 128).T)
    m["sd"] = np.ascontiguousarray(I["L0_ffn_s_down"].reshape(16, 128).T)
    Wq = I["L0_attn_b_Wq"]; Wk = I["L0_attn_b_Wk"]; Wv = I["L0_attn_b_Wv"]
    m["wqb"] = np.ascontiguousarray(Wq.transpose(1, 0, 2).reshape(E, E))
    m["wkb"] = np.ascontiguousarray(Wk.transpose(1, 0, 2).reshape(E, 512))
    m["wvb"] = np.ascontiguousarray(Wv.transpose(1, 0, 2).reshape(E, 512))
    m["bqb"] = np.ascontiguousarray(I["L0_attn_b_bq"].T)
    m["bkb"] = np.ascontiguousarray(I["L0_attn_b_bk"].T)
    m["bvb"] = np.ascontiguousarray(I["L0_attn_b_bv"].reshape(1, 512))
    rr = np.arange(128)[:, None]
    uu = np.arange(3072)[None, :]
    m["mgm"] = np.where(uu <= rr + 2048, 0.0, NEG).astype(np.float32)
    se = 0 if j0 % 2 == 0 else 1
    jE = j0 if se == 0 else j1
    so = 1 - se
    jO = j1 if se == 0 else j0
    rank = lambda b, j: j if b == 0 else 7 - j
    p = np.arange(128, dtype=np.int64)
    idx = np.zeros((968, 128), np.int64)
    R_ = p3_rows()
    def fill_unit(kind, sp, Tn, b, blocks):
        idx[R_[(kind, sp, "q")]] = Tn * 128 + p
        nk = 4 if kind == "E" else 8
        for i in range(nk):
            g = i if kind == "E" else i // 2
            half = 0 if kind == "E" else i % 2
            r = blocks[half]
            idx[R_[(kind, sp, "K", i)]] = 2 * (512 * r + 128 * g + p) + b
        nv = 8 if kind == "E" else 16
        for i in range(nv):
            half = 0 if kind == "E" else i // 8
            ch = i if kind == "E" else i % 8
            r = blocks[half]
            idx[R_[(kind, sp, "V", i)]] = 2048 * r + 1024 * b + 128 * ch + p
        idx[R_[(kind, sp, "st")]] = Tn * 128 + p
        if kind == "F":
            idx[R_[(kind, sp, "stg")]] = Tn * 128 + p
    for sp in range(8):
        fill_unit("E", sp, se * 8 + sp, se, [rank(se, jE)])
        fill_unit("O", sp, so * 8 + sp, so, [rank(so, jO - 1), rank(so, jO)])
    def kcount(Tn):
        slot = Tn // 8
        j = j0 if slot == 0 else j1
        return j // 2 if j % 2 == 0 else (j - 1) // 2
    u = 0
    for k in range(4):
        for Tn in range(16):
            if kcount(Tn) > k:
                b = Tn // 8
                fill_unit("F", u, Tn, b, [rank(b, 2 * k), rank(b, 2 * k + 1)])
                u += 1
    assert u == 24, u
    m["idxs"] = np.ascontiguousarray(idx.T.astype(np.int32))
    return m


_NC_CACHE = {}

def kernel(**inputs):
    if "nc" not in _NC_CACHE:
        _NC_CACHE["nc"] = build_program()
    nc = _NC_CACHE["nc"]
    in_maps = [_prep_core(c, inputs) for c in range(8)]
    res = run_bass_kernel_spmd(nc, in_maps, core_ids=list(range(8)))
    full = np.zeros((B, S, E), np.float32)
    for c in range(8):
        o = res.results[c]["out"]
        full[0, c * BLK:(c + 1) * BLK] = o[:, 0:BLK].T
        full[1, (7 - c) * BLK:(8 - c) * BLK] = o[:, BLK:2 * BLK].T
    return full


# version 06 (final)
# speedup vs baseline: 1.0026x; 1.0026x over previous
import numpy as np
import ml_dtypes
from contextlib import ExitStack
import concourse.bass as bass
import concourse.mybir as mybir
from concourse.bass_utils import run_bass_kernel_spmd

F32 = mybir.dt.float32
BF16 = mybir.dt.bfloat16
I32 = mybir.dt.int32
AF = mybir.ActivationFunctionType
ALU = mybir.AluOpType
AX = mybir.AxisListType.X

E = 2048
HID = 5632
S = 8192
B = 2
NH = 16
DH = 128
G = 4
WIN = 512
BLK = 1024
NCHUNK = E // 128
SCALE_A = 1.0 / float(np.sqrt(DH))
NEG = -1e30

STAGE = 4
SERIALIZE = set()


def p3_rows():
    rows = {}
    n = 0
    for sp in range(8):
        rows[("E", sp, "q")] = n; n += 1
        for i in range(4): rows[("E", sp, "K", i)] = n; n += 1
        for i in range(8): rows[("E", sp, "V", i)] = n; n += 1
        rows[("E", sp, "st")] = n; n += 1
    for sp in range(8):
        rows[("O", sp, "q")] = n; n += 1
        for i in range(8): rows[("O", sp, "K", i)] = n; n += 1
        for i in range(16): rows[("O", sp, "V", i)] = n; n += 1
        rows[("O", sp, "st")] = n; n += 1
    for u in range(24):
        rows[("F", u, "q")] = n; n += 1
        rows[("F", u, "stg")] = n; n += 1
        for i in range(8): rows[("F", u, "K", i)] = n; n += 1
        for i in range(16): rows[("F", u, "V", i)] = n; n += 1
        rows[("F", u, "st")] = n; n += 1
    assert n == 968
    return rows

P3ROWS = p3_rows()


SEMS = ["sd", "sg", "sv", "sa", "sp", "so",
        "dx0", "dx1", "dx2", "dwv", "dwq0", "dwq1", "dwo", "dmk", "dmisc",
        "dg0", "dg1", "dd0", "dd1", "dcw", "ds0", "ds1",
        "dk0", "dk1", "dv0", "dv1", "dq0", "dq1", "dst0", "dst1"]


class Em:

    def __init__(self, eng, proxy, sems, sems_cc):
        self.eng = eng
        self.p = proxy
        self.sems = sems
        self.sems_cc = sems_cc
        self.cnt = dict.fromkeys(SEMS, 0)

    sect = "top"

    def do(self, eng, incs, fn):
        ser = ("all" in SERIALIZE) or (Em.sect in SERIALIZE)
        if ser and not incs:
            incs = [({"sync": "sd", "gpsimd": "sg", "vector": "sv",
                      "scalar": "sa", "tensor": "sp"}[eng],
                     16 if eng in ("sync", "gpsimd") else 1)]
        if eng == self.eng:
            if ser:
                self.barrier()
            ins = fn(self.p)
            for s, k in incs:
                ins.then_inc(self.sems[s], k)
        tok = None
        for s, k in incs:
            self.cnt[s] += k
            tok = (s, self.cnt[s])
        return tok

    def fdo(self, eng, sem, fn):
        t = self.do(eng, [(sem, 1)], fn)
        self.wait(eng, t)
        return t

    def wait(self, eng, *toks):
        for tok in toks:
            if tok is None:
                continue
            s, v = tok
            if eng == self.eng:
                self.p.wait_ge(self.sems[s], v)

    def barrier(self):
        caps = [(s, self.cnt[s]) for s in SEMS if self.cnt[s] > 0]
        for s, v in caps:
            if self.eng is not None:
                self.p.wait_ge(self.sems[s], v)

    def cc(self, idx, ins_, outs_):
        if self.eng == "gpsimd":
            self.p.collective_compute(
                "AllGather", ALU.bypass, replica_groups=[list(range(8))],
                ins=[ins_], outs=[outs_]).then_inc(self.sems_cc[idx])

    def cc_wait(self, eng, idx):
        if eng == self.eng:
            self.p.wait_ge(self.sems_cc[idx], 1)


def build_program():
    nc = bass.Bass("TRN2", target_bir_lowering=False, debug=False)

    dram = {}
    def din(name, shape, dt):
        dram[name] = nc.dram_tensor(name, shape, dt, kind="ExternalInput").ap()
        return dram[name]
    def dout(name, shape, dt):
        dram[name] = nc.dram_tensor(name, shape, dt, kind="ExternalOutput").ap()
        return dram[name]
    dram_h = {}
    def dint(name, shape, dt):
        t = nc.dram_tensor(name, shape, dt)
        dram[name] = t.ap()
        dram_h[name] = t
        return dram[name]

    xa = din("xa", [E, 1536], F32)
    xb = din("xb", [E, 1536], F32)
    wq_a = din("wq_a", [E, E], F32)
    wk_a = din("wk_a", [E, E], F32)
    wv_a = din("wv_a", [E, E], F32)
    wout = din("wout", [E, E], F32)
    mska = din("mska", [16, 128, 640], F32)
    identbf = din("identbf", [128, 128], BF16)
    wg = din("wg", [E, HID], I32)
    wu = din("wu", [E, HID], I32)
    wd = din("wd", [HID, E], I32)
    sg_i = din("sg", [128, 44], F32)
    su_i = din("su", [128, 44], F32)
    sd_i = din("sd", [128, 16], F32)
    wqb = din("wqb", [E, E], F32)
    wkb = din("wkb", [E, 512], F32)
    wvb = din("wvb", [E, 512], F32)
    bqb = din("bqb", [128, 16], F32)
    bkb = din("bkb", [128, 4], F32)
    bvb = din("bvb", [1, 512], F32)
    mgm = din("mgm", [128, 3072], F32)
    idxs_i = din("idxs", [128, 968], I32)
    out = dout("out", [E, 2 * BLK], F32)
    DBG = (STAGE == 1)
    if STAGE == 2:
        dbg_x1 = dout("dbg_x1", [E, 2 * BLK], F32)
        dbg_h2 = dout("dbg_h2", [128, 32768], F32)
    if STAGE == 3:
        dbg_st = dout("dbg_st", [2048, 2112], F32)
        dbg_kg = dout("dbg_kg", [512, 2048], F32)
        dbg_vg = dout("dbg_vg", [512, 512], F32)
        dbg_qs = dout("dbg_qs", [512, 2048], F32)
        dbg_kc = dout("dbg_kc", [512, 2048], F32)
        dbg_vc = dout("dbg_vc", [512, 512], F32)
    if DBG:
        dbg_h = dout("dbg_h", [128, 16 * 1536], F32)
        dbg_avf = dout("dbg_avf", [128, 16 * 1024], F32)
        dbg_q = dout("dbg_q", [128, 1024], F32)
        dbg_k = dout("dbg_k", [128, 1536], F32)
        dbg_p = dout("dbg_p", [128, 640], F32)
        dbg_pt = dout("dbg_pt", [128, 640], F32)
        dbg_v = dout("dbg_v", [128, 10240], F32)
    x1T = dint("x1T", [E, 2 * BLK], F32)
    x2T = dint("x2T", [E, 2 * BLK], F32)
    wgbf = dint("wgbf", [E, HID], BF16)
    wubf = dint("wubf", [E, HID], BF16)
    wdbf = dint("wdbf", [HID, E], BF16)
    x3T = dint("x3T", [E, 2 * BLK], F32)
    qspill = dint("qspill", [16 * 128, 16 * 128], BF16)
    kcon = dint("kcon", [512, 2048], BF16)
    vcon = dint("vcon", [2048, 512], BF16)
    kg = dint("kg", [4096, 2048], BF16)
    vg = dint("vg", [16384, 512], BF16)
    stt_d = dint("stt_d", [16 * 128, 2112], F32)
    y3t = dint("y3t", [16 * 128, 2048], BF16)

    es = ExitStack()
    sb = lambda n, shp, dt: es.enter_context(nc.sbuf_tensor(n, shp, dt))
    ps_ = lambda n, shp, dt: es.enter_context(nc.psum_tensor(n, shp, dt))

    with es:
        AR = sb("AR", [128, 86528], BF16)
        FA = sb("FA", [128, 6400], F32)
        def bview(off, n):
            return AR[:, off:off + n]
        def fview(off, n):
            return FA[:, off:off + n]
        hslab = bview(0, 24576)
        vslab = bview(24576, 24576)
        avf = bview(49152, 16384)
        wst = bview(65536, 8192)
        maskb = bview(73728, 5120)
        wos = bview(73728, 2048)
        psb = bview(78848, 1280)
        ptsb = bview(80128, 1280)
        qsb = bview(81408, 2048)
        ksb = bview(83456, 3072)
        xsq = bview(78848, 1024)
        xbf = xsq
        h2slab = bview(0, 32768)
        zslab = bview(32768, 22528)
        wgu = bview(55296, 8192)
        wds = bview(63488, 11264)
        gsil = bview(74752, 2048)
        xbf2 = bview(76800, 1024)
        xsq2 = bview(77824, 1024)
        KS = [bview(32768, 8192), bview(40960, 8192)]
        VS = [bview(49152, 8192), bview(57344, 8192)]
        QS = [bview(65536, 2048), bview(67584, 2048)]
        PS3 = [bview(69632, 2048), bview(71680, 2048)]
        PT3 = [bview(73728, 2048), bview(75776, 2048)]
        mgmb = bview(77824, 3072)
        y3b = [bview(80896, 2048), bview(82944, 2048)]
        xt = fview(0, 1024)
        x1c = fview(1024, 1024)
        scl = fview(2048, 128)
        STS = [fview(2176, 2112), fview(4288, 2112)]
        hrows = FA[0:1, 4288:6336]
        hrowb = AR[0:1, 80128:81664]
        cols = sb("cols", [128, 64], F32)
        bqcol = cols[:, 16:32]
        bkcol = cols[:, 32:36]
        onescol = sb("onescol", [128, 1], BF16)
        onesrow = sb("onesrow", [1, 128], BF16)
        idsb = sb("idsb", [128, 128], BF16)
        idxsb = sb("idxsb", [128, 968], I32)

        psA = ps_("psA", [128, 1024], F32)
        psB = ps_("psB", [128, 1024], F32)
        pt = ps_("pt", [128, 2048], BF16)
        av = ps_("av", [128, 512], F32)
        m1 = ps_("m1", [128, 512], F32)

        sems = {s_: es.enter_context(nc.semaphore(s_)) for s_ in SEMS}
        sems_cc = [es.enter_context(nc.semaphore(f"cc{i}")) for i in range(2)]

        def program(EM: Em):
            T = {}

            t_id = EM.do("sync", [("dmisc", 16)], lambda p: p.dma_start(out=idsb[:, :], in_=identbf))
            EM.do("vector", [], lambda p: p.memset(onescol[:, :], 1.0))
            t_ones = EM.do("vector", [("sv", 1)], lambda p: p.memset(onesrow[:, :], 1.0))
            EM.wait("tensor", t_id, t_ones)

            def ln_to_h(src_ap, ncols, dst_stride, dst=None, bx=None, bq=None):
                if dst is None: dst = hslab
                if bq is None: bq = xsq
                ngr = ncols // 512
                t_h = None
                sect0 = Em.sect
                for tg in range(ngr):
                    Em.sect = sect0 + "_stats"
                    t_mm2 = None
                    for dc in range(NCHUNK):
                        sl = dc % 2
                        bsl = dc % 2
                        dreg = dc * dst_stride + tg * 512
                        EM.wait("sync", T.get(("xt", sl)))
                        t_x = EM.do("sync", [(f"dx{sl}", 16)], lambda p, dc=dc, sl=sl, tg=tg: p.dma_start(
                            out=xt[:, sl * 512:(sl + 1) * 512],
                            in_=src_ap[dc * 128:(dc + 1) * 128, tg * 512:(tg + 1) * 512]))
                        EM.wait("vector", t_x, T.get(("lncast", tg % 2, dc)))
                        t_bf = EM.do("vector", [("sv", 1)], lambda p, sl=sl, dreg=dreg: p.tensor_copy(
                            out=dst[:, dreg:dreg + 512], in_=xt[:, sl * 512:(sl + 1) * 512]))
                        T[("xt", sl)] = t_bf
                        EM.wait("scalar", t_bf, T.get(("xsq", bsl)))
                        t_sq = EM.do("scalar", [("sa", 1)], lambda p, bsl=bsl, dreg=dreg: p.activation(
                            out=bq[:, bsl * 512:(bsl + 1) * 512], in_=dst[:, dreg:dreg + 512],
                            func=AF.Square))
                        if dc == 0:
                            EM.wait("tensor", T.get("lnpsguard"))
                        EM.wait("tensor", t_bf, t_sq)
                        EM.do("tensor", [], lambda p, dc=dc, dreg=dreg: p.matmul(
                            out=m1[0:1, 0:512], lhsT=onescol[:, :],
                            rhs=dst[:, dreg:dreg + 512],
                            start=(dc == 0), stop=(dc == NCHUNK - 1)))
                        t_mm2 = EM.do("tensor", [("sp", 1)], lambda p, dc=dc, bsl=bsl: p.matmul(
                            out=av[0:1, 0:512], lhsT=onescol[:, :],
                            rhs=bq[:, bsl * 512:(bsl + 1) * 512],
                            start=(dc == 0), stop=(dc == NCHUNK - 1)))
                        T[("xsq", bsl)] = t_mm2
                    Em.sect = sect0 + "_rows"
                    R = lambda i: hrows[0:1, i * 512:(i + 1) * 512]
                    EM.wait("vector", t_mm2)
                    EM.fdo("vector", "sv", lambda p: p.tensor_scalar_mul(
                        out=R(0), in0=m1[0:1, 0:512], scalar1=1.0 / E))
                    t_r = EM.fdo("vector", "sv", lambda p: p.tensor_scalar_mul(
                        out=R(1), in0=av[0:1, 0:512], scalar1=1.0 / E))
                    EM.wait("scalar", t_r)
                    t_m2 = EM.do("scalar", [("sa", 1)], lambda p: p.activation(
                        out=R(2), in_=R(0), func=AF.Square))
                    EM.wait("vector", t_m2)
                    EM.fdo("vector", "sv", lambda p: p.tensor_sub(
                        out=R(3), in0=R(1), in1=R(2)))
                    t_var = EM.fdo("vector", "sv", lambda p: p.tensor_scalar_add(
                        out=R(3), in0=R(3), scalar1=1e-5))
                    EM.wait("scalar", t_var)
                    t_sq2 = EM.do("scalar", [("sa", 1)], lambda p: p.activation(
                        out=R(1), in_=R(3), func=AF.Sqrt))
                    EM.wait("vector", t_sq2)
                    EM.fdo("vector", "sv", lambda p: p.reciprocal(out=R(2), in_=R(1)))
                    EM.fdo("vector", "sv", lambda p: p.tensor_mul(out=R(3), in0=R(0), in1=R(2)))
                    EM.fdo("vector", "sv", lambda p: p.tensor_scalar_mul(
                        out=R(1), in0=R(3), scalar1=-1.0))
                    EM.fdo("vector", "sv", lambda p: p.tensor_copy(
                        out=hrowb[0:1, 0:512], in_=R(2)))
                    EM.fdo("vector", "sv", lambda p: p.tensor_copy(
                        out=R(3), in_=hrowb[0:1, 0:512]))
                    EM.fdo("vector", "sv", lambda p: p.tensor_sub(
                        out=R(0), in0=R(2), in1=R(3)))
                    EM.fdo("vector", "sv", lambda p: p.tensor_copy(
                        out=hrowb[0:1, 512:1024], in_=R(0)))
                    t_rows = EM.fdo("vector", "sv", lambda p: p.tensor_copy(
                        out=hrowb[0:1, 1024:1536], in_=R(1)))
                    EM.wait("tensor", t_rows, T.get("lnbcguard"))
                    EM.do("tensor", [], lambda p: p.matmul(
                        out=psA[:, 0:512], lhsT=onesrow[0:1, :], rhs=hrowb[0:1, 0:512],
                        start=True, stop=False))
                    EM.do("tensor", [], lambda p: p.matmul(
                        out=psA[:, 0:512], lhsT=onesrow[0:1, :], rhs=hrowb[0:1, 512:1024],
                        start=False, stop=True))
                    t_bc = EM.do("tensor", [("sp", 1)], lambda p: p.matmul(
                        out=psB[:, 0:512], lhsT=onesrow[0:1, :], rhs=hrowb[0:1, 1024:1536],
                        start=True, stop=True))
                    Em.sect = sect0 + "_norm"
                    for dc in range(NCHUNK):
                        dreg = dc * dst_stride + tg * 512
                        xs = dc % 2
                        EM.wait("vector", t_bc, T.get(("x1c", xs)))
                        EM.fdo("vector", "sv", lambda p, dreg=dreg, xs=xs: p.tensor_mul(
                            out=x1c[:, xs * 512:(xs + 1) * 512],
                            in0=dst[:, dreg:dreg + 512], in1=psA[:, 0:512]))
                        t_h = EM.fdo("vector", "sv", lambda p, dreg=dreg, xs=xs: p.tensor_add(
                            out=dst[:, dreg:dreg + 512],
                            in0=x1c[:, xs * 512:(xs + 1) * 512], in1=psB[:, 0:512]))
                        T[("x1c", xs)] = t_h
                        T[("lncast", tg % 2, dc)] = None
                        T[("lncast", (tg + 1) % 2, dc)] = t_h
                    T["lnpsguard"] = t_h
                    T["lnbcguard"] = t_h
                Em.sect = sect0
                return t_h

            def attn_a_slot(slot, src_ap):
                t_ln = ln_to_h(src_ap, 1536, 1536)
                t_vdone = None
                for nb in range(4):
                    bsl = 0
                    EM.wait("gpsimd", T.get(("wstv", bsl)))
                    t_w = EM.do("gpsimd", [("dwv", 16)], lambda p, nb=nb: p.dma_start(
                        out=wst[:, 0:8192].rearrange("p (c n) -> p c n", n=512),
                        in_=wv_a[:, nb * 512:(nb + 1) * 512].rearrange("(c p) n -> p c n", p=128)))
                    EM.wait("tensor", t_w, t_ln)
                    for tt in range(12):
                        bank = tt % 2
                        pdst = psA if bank == 0 else psB
                        EM.wait("tensor", T.get(("vev", bank)))
                        t_mm = None
                        for dc in range(NCHUNK):
                            t_mm = EM.do("tensor", [("sp", 1)] if dc == NCHUNK - 1 else [],
                                         lambda p, tt=tt, dc=dc, bsl=bsl, pdst=pdst: p.matmul(
                                out=pdst[:, 0:512],
                                lhsT=hslab[:, dc * 1536 + tt * 128: dc * 1536 + tt * 128 + 128],
                                rhs=wst[:, dc * 512: dc * 512 + 512],
                                start=(dc == 0), stop=(dc == NCHUNK - 1)))
                        EM.wait("scalar", t_mm)
                        t_ev = EM.do("scalar", [("sa", 1)], lambda p, tt=tt, nb=nb, pdst=pdst: p.activation(
                            out=vslab[:, tt * 2048 + nb * 512: tt * 2048 + nb * 512 + 512],
                            in_=pdst[:, 0:512], func=AF.Copy))
                        T[("vev", bank)] = t_ev
                        t_vdone = t_ev
                    T[("wstv", bsl)] = t_vdone
                T[("wstqk", 0)] = t_vdone
                T[("wstqk", 1)] = t_vdone

                t_msk = EM.do("gpsimd", [("dmk", 16)], lambda p, slot=slot: p.dma_start(
                    out=maskb[:, :].rearrange("p (q n) -> p q n", n=640),
                    in_=mska[slot * 8:(slot + 1) * 8].rearrange("q r n -> r q n")))

                for h in range(NH):
                    hb = h % 2
                    EM.wait("gpsimd", T.get(("wstqk", hb)))
                    t_wq = EM.do("gpsimd", [(f"dwq{hb}", 16)], lambda p, h=h, hb=hb: p.dma_start(
                        out=wst[:, hb * 4096:hb * 4096 + 2048].rearrange("p (c n) -> p c n", n=128),
                        in_=wq_a[:, h * 128:(h + 1) * 128].rearrange("(c p) n -> p c n", p=128)))
                    t_wk = EM.do("gpsimd", [(f"dwq{hb}", 16)], lambda p, h=h, hb=hb: p.dma_start(
                        out=wst[:, hb * 4096 + 2048:hb * 4096 + 4096].rearrange("p (c n) -> p c n", n=128),
                        in_=wk_a[:, h * 128:(h + 1) * 128].rearrange("(c p) n -> p c n", p=128)))
                    EM.wait("tensor", t_wq, t_wk, T.get(("qk", hb)), t_vdone,
                            T.get(("scps", 0)), T.get(("scps", 1)))
                    t_qe = None
                    for tg in range(2):
                        pdst = psA if tg == 0 else psB
                        t_mm = None
                        for dc in range(NCHUNK):
                            t_mm = EM.do("tensor", [("sp", 1)] if dc == NCHUNK - 1 else [],
                                         lambda p, tg=tg, dc=dc, hb=hb, pdst=pdst: p.matmul(
                                out=pdst[:, 0:512],
                                lhsT=wst[:, hb * 4096 + dc * 128: hb * 4096 + dc * 128 + 128],
                                rhs=hslab[:, dc * 1536 + 512 + tg * 512: dc * 1536 + 512 + tg * 512 + 512],
                                start=(dc == 0), stop=(dc == NCHUNK - 1)))
                        EM.wait("scalar", t_mm)
                        t_qe = EM.do("scalar", [("sa", 1)], lambda p, tg=tg, hb=hb, pdst=pdst: p.activation(
                            out=qsb[:, hb * 1024 + tg * 512: hb * 1024 + tg * 512 + 512],
                            in_=pdst[:, 0:512], func=AF.Copy, scale=SCALE_A))
                    t_ke = None
                    EM.wait("tensor", t_qe)
                    for tg in range(3):
                        pdst = [psA, psB, psA][tg]
                        if tg == 2:
                            EM.wait("tensor", t_ke)
                        t_mm = None
                        for dc in range(NCHUNK):
                            t_mm = EM.do("tensor", [("sp", 1)] if dc == NCHUNK - 1 else [],
                                         lambda p, tg=tg, dc=dc, hb=hb, pdst=pdst: p.matmul(
                                out=pdst[:, 0:512],
                                lhsT=wst[:, hb * 4096 + 2048 + dc * 128: hb * 4096 + 2048 + dc * 128 + 128],
                                rhs=hslab[:, dc * 1536 + tg * 512: dc * 1536 + tg * 512 + 512],
                                start=(dc == 0), stop=(dc == NCHUNK - 1)))
                        EM.wait("scalar", t_mm)
                        t_ke = EM.do("scalar", [("sa", 1)], lambda p, tg=tg, hb=hb, pdst=pdst: p.activation(
                            out=ksb[:, hb * 1536 + tg * 512: hb * 1536 + tg * 512 + 512],
                            in_=pdst[:, 0:512], func=AF.Copy))
                    T[("wstqk", hb)] = t_ke
                    t_kq_done = t_ke

                    t_s_last = None
                    for qt in range(8):
                        it = h * 8 + qt
                        par = it % 2
                        sc = psA if par == 0 else psB
                        avp = av if par == 0 else m1
                        pto = 0 if par == 0 else 1024
                        EM.wait("tensor", t_kq_done, T.get(("scps", par)), t_msk)
                        EM.do("tensor", [], lambda p, qt=qt, hb=hb, sc=sc: p.matmul(
                            out=sc[:, 0:512],
                            lhsT=qsb[:, hb * 1024 + qt * 128: hb * 1024 + qt * 128 + 128],
                            rhs=ksb[:, hb * 1536 + qt * 128: hb * 1536 + qt * 128 + 512],
                            start=True, stop=True))
                        t_s = EM.do("tensor", [("sp", 1)], lambda p, qt=qt, hb=hb, sc=sc: p.matmul(
                            out=sc[:, 512:640],
                            lhsT=qsb[:, hb * 1024 + qt * 128: hb * 1024 + qt * 128 + 128],
                            rhs=ksb[:, hb * 1536 + qt * 128 + 512: hb * 1536 + qt * 128 + 640],
                            start=True, stop=True))
                        t_s_last = t_s
                        EM.wait("vector", t_s)
                        EM.fdo("vector", "sv", lambda p, qt=qt, sc=sc: p.tensor_add(
                            out=sc[:, 0:640], in0=sc[:, 0:640], in1=maskb[:, qt * 640:(qt + 1) * 640]))
                        t_mx = EM.do("vector", [("sv", 1)], lambda p, sc=sc: p.reduce_max(
                            out=cols[:, 0:1], in_=sc[:, 0:640], axis=AX, negate=True))
                        EM.wait("scalar", t_mx, T.get(("psb", par)))
                        t_e = EM.do("scalar", [("sa", 1)], lambda p, par=par, sc=sc: p.activation(
                            out=psb[:, par * 640:(par + 1) * 640], in_=sc[:, 0:640], func=AF.Exp,
                            bias=cols[:, 0:1]))
                        T[("scps", par)] = t_e
                        EM.wait("vector", t_e)
                        EM.fdo("vector", "sv", lambda p, par=par: p.reduce_sum(
                            out=cols[:, 1:2], in_=psb[:, par * 640:(par + 1) * 640], axis=AX))
                        EM.fdo("vector", "sv", lambda p: p.reciprocal(out=cols[:, 2:3], in_=cols[:, 1:2]))
                        t_pn = EM.do("vector", [("sv", 1)], lambda p, par=par: p.tensor_scalar_mul(
                            out=psb[:, par * 640:(par + 1) * 640],
                            in0=psb[:, par * 640:(par + 1) * 640], scalar1=cols[:, 2:3]))
                        EM.wait("tensor", t_pn, T.get(("ptps", par)))
                        t_t = None
                        for i in range(5):
                            t_t = EM.do("tensor", [("sp", 1)] if i == 4 else [],
                                        lambda p, i=i, par=par, pto=pto: p.transpose(
                                out=pt[:, pto + i * 128: pto + i * 128 + 128],
                                in_=psb[:, par * 640 + i * 128: par * 640 + i * 128 + 128],
                                identity=idsb[:, :]))
                        T[("psb", par)] = t_t
                        EM.wait("vector", t_t, T.get(("ptsb", par)))
                        t_pte = EM.do("vector", [("sv", 1)], lambda p, par=par, pto=pto: p.tensor_copy(
                            out=ptsb[:, par * 640:(par + 1) * 640], in_=pt[:, pto: pto + 640]))
                        T[("ptps", par)] = t_pte
                        EM.wait("tensor", t_pte, T.get(("avps", par)))
                        t_av = None
                        for i in range(5):
                            t_av = EM.do("tensor", [("sp", 1)] if i == 4 else [],
                                         lambda p, i=i, qt=qt, h=h, par=par, avp=avp: p.matmul(
                                out=avp[:, 0:128],
                                lhsT=vslab[:, (qt + i) * 2048 + h * 128: (qt + i) * 2048 + h * 128 + 128],
                                rhs=ptsb[:, par * 640 + i * 128: par * 640 + i * 128 + 128],
                                start=(i == 0), stop=(i == 4)))
                        T[("ptsb", par)] = t_av
                        EM.wait("scalar", t_av)
                        t_avf = EM.do("scalar", [("sa", 1)], lambda p, h=h, qt=qt, avp=avp: p.activation(
                            out=avf[:, h * 1024 + qt * 128: h * 1024 + qt * 128 + 128],
                            in_=avp[:, 0:128], func=AF.Copy))
                        T[("avps", par)] = t_avf
                    T[("qk", hb)] = t_s_last
                t_attn_done = T[("avps", (NH * 8 - 1) % 2)]

                for ec in range(NCHUNK):
                    eb = 0
                    EM.wait("gpsimd", T.get(("wos", eb)), t_attn_done if ec == 0 else None)
                    t_wo = EM.do("gpsimd", [("dwo", 16)], lambda p, ec=ec: p.dma_start(
                        out=wos[:, 0:2048].rearrange("p (c n) -> p c n", n=128),
                        in_=wout[:, ec * 128:(ec + 1) * 128].rearrange("(c p) n -> p c n", p=128)))
                    t_mm_last = None
                    for tg in range(2):
                        it = ec * 2 + tg
                        par = it % 2
                        pdst = psA if par == 0 else psB
                        EM.wait("tensor", t_wo, t_attn_done, T.get(("ops", par)),
                                T.get(("scps", 0)), T.get(("scps", 1)))
                        t_mm = None
                        for dc in range(NCHUNK):
                            t_mm = EM.do("tensor", [("sp", 1)] if dc == NCHUNK - 1 else [],
                                         lambda p, tg=tg, dc=dc, pdst=pdst: p.matmul(
                                out=pdst[:, 0:512],
                                lhsT=wos[:, dc * 128: dc * 128 + 128],
                                rhs=avf[:, dc * 1024 + tg * 512: dc * 1024 + tg * 512 + 512],
                                start=(dc == 0), stop=(dc == NCHUNK - 1)))
                        t_mm_last = t_mm
                        xs = par
                        EM.wait("sync", T.get(("xt", xs)))
                        t_x0 = EM.do("sync", [(f"dx{xs}", 16)], lambda p, ec=ec, tg=tg, xs=xs: p.dma_start(
                            out=xt[:, xs * 512:(xs + 1) * 512],
                            in_=src_ap[ec * 128:(ec + 1) * 128, 512 + tg * 512: 512 + tg * 512 + 512]))
                        EM.wait("vector", t_mm, t_x0, T.get(("x1c", xs)))
                        t_add = EM.do("vector", [("sv", 1)], lambda p, xs=xs, pdst=pdst: p.tensor_add(
                            out=x1c[:, xs * 512:(xs + 1) * 512],
                            in0=pdst[:, 0:512], in1=xt[:, xs * 512:(xs + 1) * 512]))
                        T[("ops", par)] = t_add
                        T[("xt", xs)] = t_add
                        EM.wait("sync", t_add)
                        t_w1 = EM.do("sync", [(f"ds{xs}", 16)], lambda p, ec=ec, tg=tg, xs=xs, slot=slot: p.dma_start(
                            out=x1T[ec * 128:(ec + 1) * 128,
                                    slot * 1024 + tg * 512: slot * 1024 + tg * 512 + 512],
                            in_=x1c[:, xs * 512:(xs + 1) * 512]))
                        if STAGE == 1:
                            t_w1 = EM.do("sync", [(f"ds{xs}", 16)], lambda p, ec=ec, tg=tg, xs=xs, slot=slot: p.dma_start(
                                out=out[ec * 128:(ec + 1) * 128,
                                        slot * 1024 + tg * 512: slot * 1024 + tg * 512 + 512],
                                in_=x1c[:, xs * 512:(xs + 1) * 512]))
                        T[("x1c", xs)] = t_w1
                    T[("wos", eb)] = t_mm_last

            def ffn_block(xsrcT, xdstT, first_pass, write_out):
                Em.sect = "ln2"
                t_ln = ln_to_h(xsrcT, 2048, 2048, dst=h2slab, bx=xbf2, bq=xsq2)
                if first_pass:
                    t_sc = EM.do("sync", [("dmisc", 16)], lambda p: p.dma_start(
                        out=scl[:, 0:44], in_=sg_i))
                    t_sc = EM.do("sync", [("dmisc", 16)], lambda p: p.dma_start(
                        out=scl[:, 44:88], in_=sg_i)) if False else EM.do("sync", [("dmisc", 16)], lambda p: p.dma_start(
                        out=scl[:, 44:88], in_=su_i))
                    t_sc = EM.do("sync", [("dmisc", 16)], lambda p: p.dma_start(
                        out=scl[:, 88:104], in_=sd_i))
                    T["sclt"] = t_sc
                EM.wait("scalar", T.get("sclt"))
                EM.wait("vector", T.get("sclt"))
                for tg in range(4):
                    Em.sect = "gateup"
                    for hc in range(44):
                        par = hc % 2
                        if True:
                            if first_pass and tg == 0:
                                EM.wait("gpsimd", T.get(("wgur", par)), T.get(("wgucw", par)))
                                t_lg = EM.do("gpsimd", [(f"dg{par}", 16)], lambda p, hc=hc, par=par: p.dma_start(
                                    out=wgu[:, par * 2048:par * 2048 + 2048].rearrange("p (c n) -> p c n", n=128),
                                    in_=wg[:, hc * 128:(hc + 1) * 128].rearrange("(c p) n -> p c n", p=128)))
                                t_lu = EM.do("gpsimd", [(f"dg{par}", 16)], lambda p, hc=hc, par=par: p.dma_start(
                                    out=wgu[:, 4096 + par * 2048: 4096 + par * 2048 + 2048].rearrange("p (c n) -> p c n", n=128),
                                    in_=wu[:, hc * 128:(hc + 1) * 128].rearrange("(c p) n -> p c n", p=128)))
                                EM.wait("vector", t_lg, t_lu)
                                EM.fdo("vector", "sv", lambda p, par=par: p.tensor_scalar_add(
                                    out=wgu[:, par * 2048:par * 2048 + 2048],
                                    in0=wgu[:, par * 2048:par * 2048 + 2048], scalar1=-128.0))
                                t_cv = EM.fdo("vector", "sv", lambda p, par=par: p.tensor_scalar_add(
                                    out=wgu[:, 4096 + par * 2048: 4096 + par * 2048 + 2048],
                                    in0=wgu[:, 4096 + par * 2048: 4096 + par * 2048 + 2048], scalar1=-128.0))
                                EM.wait("sync", t_cv)
                                EM.do("sync", [("dcw", 16)], lambda p, hc=hc, par=par: p.dma_start(
                                    out=wgbf[:, hc * 128:(hc + 1) * 128].rearrange("(c p) n -> p c n", p=128),
                                    in_=wgu[:, par * 2048:par * 2048 + 2048].rearrange("p (c n) -> p c n", n=128)))
                                t_cw = EM.do("sync", [("dcw", 16)], lambda p, hc=hc, par=par: p.dma_start(
                                    out=wubf[:, hc * 128:(hc + 1) * 128].rearrange("(c p) n -> p c n", p=128),
                                    in_=wgu[:, 4096 + par * 2048: 4096 + par * 2048 + 2048].rearrange("p (c n) -> p c n", n=128)))
                                t_w = t_cv
                                T[("wgucw", par)] = t_cw
                                T["lastcw"] = t_cw
                            else:
                                EM.wait("sync", T.get(("wgur", par)), T.get(("wgucw", par)), T.get("lastcw"))
                                EM.do("sync", [(f"dg{par}", 16)], lambda p, hc=hc, par=par: p.dma_start(
                                    out=wgu[:, par * 2048:par * 2048 + 2048].rearrange("p (c n) -> p c n", n=128),
                                    in_=wgbf[:, hc * 128:(hc + 1) * 128].rearrange("(c p) n -> p c n", p=128)))
                                t_w = EM.do("sync", [(f"dg{par}", 16)], lambda p, hc=hc, par=par: p.dma_start(
                                    out=wgu[:, 4096 + par * 2048: 4096 + par * 2048 + 2048].rearrange("p (c n) -> p c n", n=128),
                                    in_=wubf[:, hc * 128:(hc + 1) * 128].rearrange("(c p) n -> p c n", p=128)))
                            T[("wgut", par)] = t_w
                        EM.wait("tensor", T.get(("wgut", par)),
                                t_ln, T.get(("gps", par)), T.get(("ups", par)),
                                T.get(("dps", par)))
                        t_g = None
                        for dc in range(NCHUNK):
                            t_g = EM.do("tensor", [("sp", 1)] if dc == NCHUNK - 1 else [],
                                        lambda p, dc=dc, par=par, tg=tg: p.matmul(
                                out=psA[:, par * 512:(par + 1) * 512],
                                lhsT=wgu[:, par * 2048 + dc * 128: par * 2048 + dc * 128 + 128],
                                rhs=h2slab[:, dc * 2048 + tg * 512: dc * 2048 + tg * 512 + 512],
                                start=(dc == 0), stop=(dc == NCHUNK - 1)))
                        t_u = None
                        for dc in range(NCHUNK):
                            t_u = EM.do("tensor", [("sp", 1)] if dc == NCHUNK - 1 else [],
                                        lambda p, dc=dc, par=par, tg=tg: p.matmul(
                                out=psB[:, par * 512:(par + 1) * 512],
                                lhsT=wgu[:, 4096 + par * 2048 + dc * 128: 4096 + par * 2048 + dc * 128 + 128],
                                rhs=h2slab[:, dc * 2048 + tg * 512: dc * 2048 + tg * 512 + 512],
                                start=(dc == 0), stop=(dc == NCHUNK - 1)))
                        T[("wgur", par)] = t_u
                        EM.wait("scalar", t_g, T.get(("gsil", par)))
                        t_si = EM.do("scalar", [("sa", 1)], lambda p, hc=hc, par=par: p.activation(
                            out=gsil[:, par * 512:(par + 1) * 512],
                            in_=psA[:, par * 512:(par + 1) * 512], func=AF.Silu,
                            scale=scl[:, hc:hc + 1]))
                        T[("gps", par)] = t_si
                        EM.wait("vector", t_u, t_si, T.get("zguard"))
                        t_z = EM.do("vector", [("sv", 1)], lambda p, hc=hc, par=par: p.scalar_tensor_tensor(
                            out=zslab[:, hc * 512:(hc + 1) * 512],
                            in0=psB[:, par * 512:(par + 1) * 512], scalar=scl[:, 44 + hc:44 + hc + 1],
                            in1=gsil[:, par * 512:(par + 1) * 512],
                            op0=ALU.mult, op1=ALU.mult))
                        T[("ups", par)] = t_z
                        T[("gsil", par)] = t_z
                        T[("zw", hc)] = t_z
                    t_zlast = T[("zw", 43)]
                    Em.sect = "down"
                    for ec in range(NCHUNK):
                        par = ec % 2
                        if True:
                            if first_pass and tg == 0:
                                EM.wait("gpsimd", T.get(("wdsr", par)), T.get(("wdscw", par)))
                                t_ld = EM.do("gpsimd", [(f"dd{par}", 16)], lambda p, ec=ec, par=par: p.dma_start(
                                    out=wds[:, par * 5632:par * 5632 + 5632].rearrange("p (c n) -> p c n", n=128),
                                    in_=wd[:, ec * 128:(ec + 1) * 128].rearrange("(c p) n -> p c n", p=128)))
                                EM.wait("vector", t_ld)
                                t_cv = EM.fdo("vector", "sv", lambda p, par=par: p.tensor_scalar_add(
                                    out=wds[:, par * 5632:par * 5632 + 5632],
                                    in0=wds[:, par * 5632:par * 5632 + 5632], scalar1=-128.0))
                                EM.wait("sync", t_cv)
                                t_cw = EM.do("sync", [("dcw", 16)], lambda p, ec=ec, par=par: p.dma_start(
                                    out=wdbf[:, ec * 128:(ec + 1) * 128].rearrange("(c p) n -> p c n", p=128),
                                    in_=wds[:, par * 5632:par * 5632 + 5632].rearrange("p (c n) -> p c n", n=128)))
                                t_wd = t_cv
                                T[("wdscw", par)] = t_cw
                                T["lastcw"] = t_cw
                            else:
                                EM.wait("sync", T.get(("wdsr", par)), T.get(("wdscw", par)), T.get("lastcw"))
                                t_wd = EM.do("sync", [(f"dd{par}", 16)], lambda p, ec=ec, par=par: p.dma_start(
                                    out=wds[:, par * 5632:par * 5632 + 5632].rearrange("p (c n) -> p c n", n=128),
                                    in_=wdbf[:, ec * 128:(ec + 1) * 128].rearrange("(c p) n -> p c n", p=128)))
                            T[("wdst", par)] = t_wd
                        EM.wait("tensor", T.get(("wdst", par)), t_zlast,
                                T.get(("dps", par)), T.get(("gps", par)), T.get(("ups", par)))
                        pd = psA if par == 0 else psB
                        po = 0 if par == 0 else 512
                        t_dn = None
                        for hc in range(44):
                            t_dn = EM.do("tensor", [("sp", 1)] if hc == 43 else [],
                                         lambda p, hc=hc, par=par, pd=pd, po=po: p.matmul(
                                out=pd[:, po:po + 512],
                                lhsT=wds[:, par * 5632 + hc * 128: par * 5632 + hc * 128 + 128],
                                rhs=zslab[:, hc * 512:(hc + 1) * 512],
                                start=(hc == 0), stop=(hc == 43)))
                        T[("wdsr", par)] = t_dn
                        xs = par
                        EM.wait("sync", T.get(("xt", xs)))
                        t_x1 = EM.do("sync", [(f"dx{xs}", 16)], lambda p, ec=ec, tg=tg, xs=xs: p.dma_start(
                            out=xt[:, xs * 512:(xs + 1) * 512],
                            in_=xsrcT[ec * 128:(ec + 1) * 128, tg * 512:(tg + 1) * 512]))
                        EM.wait("vector", t_dn, t_x1, T.get(("x1c", xs)))
                        t_add = EM.do("vector", [("sv", 1)], lambda p, ec=ec, xs=xs, pd=pd, po=po: p.scalar_tensor_tensor(
                            out=x1c[:, xs * 512:(xs + 1) * 512],
                            in0=pd[:, po:po + 512], scalar=scl[:, 88 + ec:88 + ec + 1],
                            in1=xt[:, xs * 512:(xs + 1) * 512],
                            op0=ALU.mult, op1=ALU.add))
                        T[("dps", par)] = t_add
                        T[("xt", xs)] = t_add
                        EM.wait("sync", t_add)
                        t_wx = None
                        if xdstT is not None:
                            t_wx = EM.do("sync", [(f"ds{xs}", 16)], lambda p, ec=ec, tg=tg, xs=xs: p.dma_start(
                                out=xdstT[ec * 128:(ec + 1) * 128, tg * 512:(tg + 1) * 512],
                                in_=x1c[:, xs * 512:(xs + 1) * 512]))
                        if write_out:
                            t_wx = EM.do("sync", [(f"ds{xs}", 16)], lambda p, ec=ec, tg=tg, xs=xs: p.dma_start(
                                out=out[ec * 128:(ec + 1) * 128, tg * 512:(tg + 1) * 512],
                                in_=x1c[:, xs * 512:(xs + 1) * 512]))
                        T[("x1c", xs)] = t_wx
                    T["zguard"] = T[("wdsr", 15 % 2)]
                Em.sect = "top"

            from concourse.bass import IndirectOffsetOnAxis as IOA
            kgr = kg.rearrange("q (b n) -> (q b) n", b=2)

            def qkv_b():
                for h in range(NH):
                    hb = h % 2
                    EM.wait("gpsimd", T.get(("wstqk", hb)))
                    t_wq = EM.do("gpsimd", [(f"dwq{hb}", 16)], lambda p, h=h, hb=hb: p.dma_start(
                        out=wst[:, hb * 4096:hb * 4096 + 2048].rearrange("p (c n) -> p c n", n=128),
                        in_=wqb[:, h * 128:(h + 1) * 128].rearrange("(c p) n -> p c n", p=128)))
                    t_qs = None
                    for tg in range(4):
                        par = tg % 2
                        pdst = psA if par == 0 else psB
                        EM.wait("tensor", t_wq, T.get(("qbps", par)))
                        t_mm = None
                        for dc in range(NCHUNK):
                            t_mm = EM.do("tensor", [("sp", 1)] if dc == NCHUNK - 1 else [],
                                         lambda p, tg=tg, dc=dc, hb=hb, pdst=pdst: p.matmul(
                                out=pdst[:, 0:512],
                                lhsT=wst[:, hb * 4096 + dc * 128: hb * 4096 + dc * 128 + 128],
                                rhs=h2slab[:, dc * 2048 + tg * 512: dc * 2048 + tg * 512 + 512],
                                start=(dc == 0), stop=(dc == NCHUNK - 1)))
                        EM.wait("vector", t_mm, T.get(("qev", par)))
                        t_qe = EM.do("vector", [("sv", 1)], lambda p, h=h, tg=tg, par=par, pdst=pdst: p.tensor_scalar(
                            out=y3b[0][:, par * 512:(par + 1) * 512],
                            in0=pdst[:, 0:512], scalar1=bqcol[:, h:h + 1], scalar2=SCALE_A,
                            op0=ALU.add, op1=ALU.mult))
                        T[("qbps", par)] = t_qe
                        EM.wait("sync", t_qe, T.get(("qsp", par)))
                        t_qs = EM.do("sync", [(f"ds{par}", 16)], lambda p, h=h, tg=tg, par=par: p.dma_start(
                            out=qspill.rearrange("(T p) (hh q) -> T p hh q", p=128, q=128)
                                [tg * 4:(tg + 1) * 4, :, h, :].rearrange("T p q -> p T q"),
                            in_=y3b[0][:, par * 512:(par + 1) * 512].rearrange("p (T q) -> p T q", q=128)))
                        T[("qev", par)] = t_qs
                        T[("qsp", par)] = t_qs
                    T[("wstqk", hb)] = t_qs
                for g in range(G):
                    hb = g % 2
                    EM.wait("gpsimd", T.get(("wstqk", hb)))
                    t_wk = EM.do("gpsimd", [(f"dwq{hb}", 16)], lambda p, g=g, hb=hb: p.dma_start(
                        out=wst[:, hb * 4096:hb * 4096 + 2048].rearrange("p (c n) -> p c n", n=128),
                        in_=wkb[:, g * 128:(g + 1) * 128].rearrange("(c p) n -> p c n", p=128)))
                    t_ks = None
                    for tg in range(4):
                        par = tg % 2
                        pdst = psA if par == 0 else psB
                        EM.wait("tensor", t_wk, T.get(("qbps", par)))
                        t_mm = None
                        for dc in range(NCHUNK):
                            t_mm = EM.do("tensor", [("sp", 1)] if dc == NCHUNK - 1 else [],
                                         lambda p, tg=tg, dc=dc, hb=hb, pdst=pdst: p.matmul(
                                out=pdst[:, 0:512],
                                lhsT=wst[:, hb * 4096 + dc * 128: hb * 4096 + dc * 128 + 128],
                                rhs=h2slab[:, dc * 2048 + tg * 512: dc * 2048 + tg * 512 + 512],
                                start=(dc == 0), stop=(dc == NCHUNK - 1)))
                        EM.wait("vector", t_mm, T.get(("qev", par)))
                        t_ke = EM.do("vector", [("sv", 1)], lambda p, g=g, par=par, pdst=pdst: p.tensor_scalar_add(
                            out=y3b[0][:, par * 512:(par + 1) * 512],
                            in0=pdst[:, 0:512], scalar1=bkcol[:, g:g + 1]))
                        T[("qbps", par)] = t_ke
                        EM.wait("sync", t_ke, T.get(("qsp", par)))
                        t_ks = EM.do("sync", [(f"ds{par}", 16)], lambda p, g=g, tg=tg, par=par: p.dma_start(
                            out=kcon[g * 128:(g + 1) * 128, tg * 512:(tg + 1) * 512],
                            in_=y3b[0][:, par * 512:(par + 1) * 512]))
                        T[("qev", par)] = t_ks
                        T[("qsp", par)] = t_ks
                    T[("wstqk", hb)] = t_ks
                EM.wait("gpsimd", T.get(("wstqk", 0)), T.get(("wstqk", 1)))
                t_wv = EM.do("gpsimd", [("dwv", 16)], lambda p: p.dma_start(
                    out=wst[:, 0:8192].rearrange("p (c n) -> p c n", n=512),
                    in_=wvb.rearrange("(c p) n -> p c n", p=128)))
                t_bv = EM.do("gpsimd", [("dmisc", 16)], lambda p: p.dma_start(
                    out=hrowb[0:1, 0:512], in_=bvb))
                for tt in range(16):
                    par = tt % 2
                    pdst = psA if par == 0 else psB
                    EM.wait("tensor", t_wv, t_bv, T.get(("qbps", par)))
                    for dc in range(NCHUNK):
                        EM.do("tensor", [], lambda p, tt=tt, dc=dc, pdst=pdst: p.matmul(
                            out=pdst[:, 0:512],
                            lhsT=h2slab[:, dc * 2048 + tt * 128: dc * 2048 + tt * 128 + 128],
                            rhs=wst[:, dc * 512:(dc + 1) * 512],
                            start=(dc == 0), stop=False))
                    t_mm = EM.do("tensor", [("sp", 1)], lambda p, pdst=pdst: p.matmul(
                        out=pdst[:, 0:512], lhsT=onesrow[0:1, :], rhs=hrowb[0:1, 0:512],
                        start=False, stop=True))
                    EM.wait("vector", t_mm, T.get(("qev", par)))
                    t_ve = EM.do("vector", [("sv", 1)], lambda p, par=par, pdst=pdst: p.tensor_copy(
                        out=y3b[0][:, par * 512:(par + 1) * 512], in_=pdst[:, 0:512]))
                    T[("qbps", par)] = t_ve
                    EM.wait("sync", t_ve, T.get(("qsp", par)))
                    t_vs = EM.do("sync", [(f"ds{par}", 16)], lambda p, tt=tt, par=par: p.dma_start(
                        out=vcon[tt * 128:(tt + 1) * 128, :],
                        in_=y3b[0][:, par * 512:(par + 1) * 512]))
                    T[("qev", par)] = t_vs
                    T[("qsp", par)] = t_vs

            def attn_b_unit(kind, sp, uidx, ucount):
                par = uidx % 2
                W = 1024 if kind == "E" else 2048
                nch = W // 128
                nsc = W // 512
                R_ = P3ROWS
                irow = lambda key: idxsb[:, R_[key]:R_[key] + 1]
                EM.wait("gpsimd", T.get(("qsr", par)), T.get(("stu", 0)), T.get(("stu", 1)))
                t_q = EM.do("gpsimd", [(f"dq{par}", 16)], lambda p: p.indirect_dma_start(
                    out=QS[par][:, :], out_offset=None, in_=qspill[:, :],
                    in_offset=IOA(ap=irow((kind, sp, "q")), axis=0)))
                if kind == "F":
                    t_st = EM.do("gpsimd", [(f"dst{par}", 16)], lambda p: p.indirect_dma_start(
                        out=STS[par][:, :], out_offset=None, in_=stt_d[:, :],
                        in_offset=IOA(ap=irow((kind, sp, "stg")), axis=0)))
                else:
                    t_st = None
                EM.wait("gpsimd", T.get(("ksr", par)))
                nk = 4 if kind == "E" else 8
                t_k = None
                for i in range(nk):
                    g = i if kind == "E" else i // 2
                    half = 0 if kind == "E" else i % 2
                    t_k = EM.do("gpsimd", [(f"dk{par}", 16)], lambda p, i=i, g=g, half=half: p.indirect_dma_start(
                        out=KS[par][:, g * 2048 + half * 1024: g * 2048 + half * 1024 + 1024],
                        out_offset=None, in_=kgr[:, :],
                        in_offset=IOA(ap=irow((kind, sp, "K", i)), axis=0)))
                EM.wait("gpsimd", T.get(("vsr", par)))
                t_v = None
                for i in range(nch):
                    t_v = EM.do("gpsimd", [(f"dv{par}", 16)], lambda p, i=i: p.indirect_dma_start(
                        out=VS[par][:, i * 512:(i + 1) * 512], out_offset=None, in_=vg[:, :],
                        in_offset=IOA(ap=irow((kind, sp, "V", i)), axis=0)))
                t_slast = None
                t_avlast = None
                for h in range(NH):
                    g2 = h // 4
                    hp = h % 2
                    avp = av if hp == 0 else m1
                    for ch2 in range(nsc // 2):
                        pch = psA if ch2 == 0 else psB
                        EM.wait("tensor", t_q, t_k, T.get(("sce", ch2)))
                        for i in range(2):
                            t_s = EM.do("tensor", [("sp", 1)] if i == 1 else [],
                                        lambda p, h=h, g2=g2, ch2=ch2, i=i, pch=pch, par=par: p.matmul(
                                out=pch[:, i * 512:(i + 1) * 512],
                                lhsT=QS[par][:, h * 128:(h + 1) * 128],
                                rhs=KS[par][:, g2 * 2048 + ch2 * 1024 + i * 512: g2 * 2048 + ch2 * 1024 + i * 512 + 512],
                                start=True, stop=True))
                        t_slast = t_s
                        if kind != "F":
                            st0 = (2048 - 128 * sp) if kind == "E" else (1024 - 128 * sp + 1024 * ch2)
                            EM.wait("vector", t_s)
                            EM.fdo("vector", "sv", lambda p, ch2=ch2, st0=st0, pch=pch: p.tensor_add(
                                out=pch[:, 0:1024], in0=pch[:, 0:1024],
                                in1=mgmb[:, st0:st0 + 1024]))
                            t_mx = EM.fdo("vector", "sv", lambda p, ch2=ch2, pch=pch: p.reduce_max(
                                out=cols[:, ch2:ch2 + 1], in_=pch[:, 0:1024], axis=AX, negate=True))
                            T[("mx", ch2)] = t_mx
                    if kind != "F":
                        if nsc == 4:
                            t_mn = EM.fdo("vector", "sv", lambda p: p.tensor_tensor(
                                out=cols[:, 0:1], in0=cols[:, 0:1], in1=cols[:, 1:2], op=ALU.min))
                        else:
                            t_mn = T[("mx", 0)]
                        t_mn = EM.fdo("vector", "sv", lambda p, h=h, par=par: p.tensor_copy(
                            out=STS[par][:, h * 132 + 129: h * 132 + 130], in_=cols[:, 0:1]))
                    else:
                        t_mn = t_st
                    t_e = None
                    for ch2 in range(nsc // 2):
                        pch = psA if ch2 == 0 else psB
                        EM.wait("scalar", t_mn, t_slast, T.get(("pse", par, ch2)),
                                T.get(("psm", par)))
                        t_e = EM.do("scalar", [("sa", 1)], lambda p, h=h, ch2=ch2, par=par, pch=pch: p.activation(
                            out=PS3[par][:, ch2 * 1024:(ch2 + 1) * 1024], in_=pch[:, 0:1024],
                            func=AF.Exp, bias=STS[par][:, h * 132 + 129: h * 132 + 130]))
                        T[("sce", ch2)] = t_e
                    EM.wait("vector", t_e)
                    t_sm = EM.fdo("vector", "sv", lambda p, par=par, W=W: p.reduce_sum(
                        out=cols[:, 2:3], in_=PS3[par][:, 0:W], axis=AX))
                    if kind == "F":
                        t_sm = EM.fdo("vector", "sv", lambda p, h=h, par=par: p.tensor_add(
                            out=STS[par][:, h * 132 + 128: h * 132 + 129],
                            in0=STS[par][:, h * 132 + 128: h * 132 + 129], in1=cols[:, 2:3]))
                    else:
                        t_sm = EM.fdo("vector", "sv", lambda p, h=h, par=par: p.tensor_copy(
                            out=STS[par][:, h * 132 + 128: h * 132 + 129], in_=cols[:, 2:3]))
                    T[("psm", par)] = t_sm
                    EM.wait("tensor", t_e, T.get("pte"))
                    t_t = None
                    for i in range(nch):
                        t_t = EM.do("tensor", [("sp", 1)] if i == nch - 1 else [],
                                    lambda p, i=i, par=par: p.transpose(
                            out=pt[:, i * 128:(i + 1) * 128],
                            in_=PS3[par][:, i * 128:(i + 1) * 128], identity=idsb[:, :]))
                    EM.wait("scalar", t_t, T.get(("pts", par)))
                    t_pte = EM.do("scalar", [("sa", 1)], lambda p, par=par, W=W: p.copy(
                        out=PT3[par][:, 0:W], in_=pt[:, 0:W]))
                    T["pte"] = t_pte
                    T[("pse", par, 0)] = t_t
                    T[("pse", par, 1)] = t_t
                    EM.wait("tensor", t_pte, t_v, T.get(("avu", hp)))
                    t_av = None
                    for i in range(nch):
                        t_av = EM.do("tensor", [("sp", 1)] if i == nch - 1 else [],
                                     lambda p, i=i, g2=g2, par=par, avp=avp: p.matmul(
                            out=avp[:, 0:128],
                            lhsT=PT3[par][:, i * 128:(i + 1) * 128],
                            rhs=VS[par][:, i * 512 + g2 * 128: i * 512 + g2 * 128 + 128],
                            start=(i == 0), stop=(i == nch - 1)))
                    t_avlast = t_av
                    T[("pts", par)] = t_av
                    EM.wait("vector", t_av, t_sm, t_mn if kind == "F" else None)
                    if kind == "F":
                        t_mg = EM.fdo("vector", "sv", lambda p, h=h, par=par, avp=avp: p.tensor_add(
                            out=STS[par][:, h * 132: h * 132 + 128],
                            in0=STS[par][:, h * 132: h * 132 + 128], in1=avp[:, 0:128]))
                    else:
                        t_mg = EM.fdo("vector", "sv", lambda p, h=h, par=par, avp=avp: p.tensor_copy(
                            out=STS[par][:, h * 132: h * 132 + 128], in_=avp[:, 0:128]))
                    T[("avu", hp)] = t_mg
                    T[("mrg", h)] = t_mg
                t_mrg_all = T[("mrg", NH - 1)]
                EM.wait("gpsimd", t_mrg_all)
                t_sc = EM.do("gpsimd", [(f"dst{par}", 16)], lambda p: p.indirect_dma_start(
                    out=stt_d[:, :], out_offset=IOA(ap=irow((kind, sp, "st")), axis=0),
                    in_=STS[par][:, :], in_offset=None))
                T[("stu", par)] = t_sc
                T[("qsr", par)] = t_slast
                T[("ksr", par)] = t_slast
                T[("vsr", par)] = t_avlast

            def p3c_finalize(t):
                par = t % 2
                EM.wait("sync", T.get(("stu", 0)), T.get(("stu", 1)), T.get(("stf", par)))
                t_st = EM.do("sync", [(f"dst{par}", 16)], lambda p, t=t: p.dma_start(
                    out=STS[par][:, :], in_=stt_d[t * 128:(t + 1) * 128, :]))
                t_y = None
                for h in range(NH):
                    EM.wait("vector", t_st, T.get(("y3w", par)) if h == 0 else None)
                    EM.fdo("vector", "sv", lambda p, h=h, par=par: p.reciprocal(
                        out=cols[:, 4:5], in_=STS[par][:, h * 132 + 128: h * 132 + 129]))
                    t_y = EM.fdo("vector", "sv", lambda p, h=h, par=par: p.tensor_scalar_mul(
                        out=y3b[par][:, h * 128:(h + 1) * 128],
                        in0=STS[par][:, h * 132: h * 132 + 128], scalar1=cols[:, 4:5]))
                T[("stf", par)] = t_y
                EM.wait("sync", t_y, T.get(("y3s", par)))
                t_ys = EM.do("sync", [(f"ds{par}", 16)], lambda p, t=t, par=par: p.dma_start(
                    out=y3t[t * 128:(t + 1) * 128, :], in_=y3b[par][:, :]))
                T[("y3w", par)] = t_ys
                T[("y3s", par)] = t_ys

            def p3d_tile(slot, qt, write_out):
                Tn = slot * 8 + qt
                tcol = slot * 1024 + qt * 128
                par = Tn % 2
                EM.wait("gpsimd", T.get(("y3r", par)))
                t_y = EM.do("gpsimd", [(f"dq{par}", 16)], lambda p, Tn=Tn, par=par: p.dma_start(
                    out=y3b[par][:, :], in_=y3t[Tn * 128:(Tn + 1) * 128, :]))
                t_tr = None
                for dc in range(NCHUNK):
                    pslot = (dc % 2) * 1024
                    EM.wait("tensor", t_y, T.get(("ptd", dc % 2)))
                    t_tr = EM.do("tensor", [("sp", 1)], lambda p, dc=dc, par=par, pslot=pslot: p.transpose(
                        out=pt[:, pslot:pslot + 128],
                        in_=y3b[par][:, dc * 128:(dc + 1) * 128], identity=idsb[:, :]))
                    sl = dc % 2
                    EM.wait("sync", T.get(("xt", sl)))
                    t_x2 = EM.do("sync", [(f"dx{sl}", 16)], lambda p, dc=dc, sl=sl, tcol=tcol: p.dma_start(
                        out=xt[:, sl * 512:sl * 512 + 128],
                        in_=x2T[dc * 128:(dc + 1) * 128, tcol:tcol + 128]))
                    xs = dc % 2
                    EM.wait("vector", t_tr, t_x2, T.get(("x1c", xs)))
                    t_add = EM.do("vector", [("sv", 1)], lambda p, dc=dc, sl=sl, xs=xs, pslot=pslot: p.tensor_add(
                        out=x1c[:, xs * 512:xs * 512 + 128],
                        in0=pt[:, pslot:pslot + 128], in1=xt[:, sl * 512:sl * 512 + 128]))
                    T[("xt", sl)] = t_add
                    T[("ptd", dc % 2)] = t_add
                    EM.wait("sync", t_add)
                    t_w = EM.do("sync", [(f"ds{xs}", 16)], lambda p, dc=dc, tcol=tcol, xs=xs: p.dma_start(
                        out=x3T[dc * 128:(dc + 1) * 128, tcol:tcol + 128],
                        in_=x1c[:, xs * 512:xs * 512 + 128]))
                    if write_out:
                        t_w = EM.do("sync", [(f"ds{xs}", 16)], lambda p, dc=dc, tcol=tcol, xs=xs: p.dma_start(
                            out=out[dc * 128:(dc + 1) * 128, tcol:tcol + 128],
                            in_=x1c[:, xs * 512:xs * 512 + 128]))
                    T[("x1c", xs)] = t_w
                T[("y3r", par)] = t_tr

            def phase3(write_out):
                ln_to_h(x2T, 2048, 2048, dst=h2slab, bx=xbf2, bq=xsq2)
                t_b = EM.do("sync", [("dmisc", 16)], lambda p: p.dma_start(out=bqcol, in_=bqb))
                t_b = EM.do("sync", [("dmisc", 16)], lambda p: p.dma_start(out=bkcol, in_=bkb))
                t_ix = EM.do("sync", [("dmisc", 16)], lambda p: p.dma_start(
                    out=idxsb[:, 0:968], in_=idxs_i[:, 0:968]))
                EM.wait("vector", t_b)
                EM.wait("gpsimd", t_ix)
                T[("qbps", 0)] = T.get("lnbcguard")
                T[("qbps", 1)] = T.get("lnbcguard")
                qkv_b()
                EM.barrier()
                if STAGE == 3:
                    for i in range(4):
                        EM.do("gpsimd", [("so", 16)], lambda p, i=i: p.dma_start(
                            out=dbg_kc[i * 128:(i + 1) * 128, :], in_=kcon[i * 128:(i + 1) * 128, :]))
                        EM.do("gpsimd", [("so", 16)], lambda p, i=i: p.dma_start(
                            out=dbg_vc[i * 128:(i + 1) * 128, :], in_=vcon[i * 128:(i + 1) * 128, :]))
                    EM.barrier()
                EM.cc(0, dram_h["kcon"].ap().opt(), dram_h["kg"].ap().opt())
                EM.cc(1, dram_h["vcon"].ap().opt(), dram_h["vg"].ap().opt())
                for e_ in ("sync", "gpsimd", "vector", "scalar", "tensor"):
                    EM.cc_wait(e_, 0)
                    EM.cc_wait(e_, 1)
                t_mg = EM.do("gpsimd", [("dmk", 16)], lambda p: p.dma_start(out=mgmb[:, :], in_=mgm))
                EM.wait("vector", t_mg)
                EM.wait("tensor", t_mg)
                uc = 0
                for sp in range(8):
                    attn_b_unit("E", sp, uc, 40); uc += 1
                for sp in range(8):
                    attn_b_unit("O", sp, uc, 40); uc += 1
                EM.barrier()
                for u in range(24):
                    attn_b_unit("F", u, uc, 40); uc += 1
                EM.barrier()
                if STAGE == 3:
                    for i in range(16):
                        EM.do("gpsimd", [("so", 16)], lambda p, i=i: p.dma_start(
                            out=dbg_st[i * 128:(i + 1) * 128, :], in_=stt_d[i * 128:(i + 1) * 128, :]))
                    for i in range(4):
                        EM.do("gpsimd", [("so", 16)], lambda p, i=i: p.dma_start(
                            out=dbg_kg[i * 128:(i + 1) * 128, :], in_=kg[i * 128:(i + 1) * 128, :]))
                        EM.do("gpsimd", [("so", 16)], lambda p, i=i: p.dma_start(
                            out=dbg_vg[i * 128:(i + 1) * 128, :], in_=vg[i * 128:(i + 1) * 128, :]))
                        EM.do("gpsimd", [("so", 16)], lambda p, i=i: p.dma_start(
                            out=dbg_qs[i * 128:(i + 1) * 128, :], in_=qspill[i * 128:(i + 1) * 128, :]))
                    EM.barrier()
                for t in range(16):
                    p3c_finalize(t)
                EM.barrier()
                for slot in range(2):
                    for qt in range(8):
                        p3d_tile(slot, qt, write_out)
                EM.barrier()

            attn_a_slot(0, xa)
            if DBG and STAGE == 1:
                EM.barrier()
                for i in range(6):
                    EM.do("gpsimd", [("so", 16)], lambda p, i=i: p.dma_start(
                        out=dbg_h[:, i * 4096:(i + 1) * 4096], in_=hslab[:, i * 4096:(i + 1) * 4096]))
                for i in range(4):
                    EM.do("gpsimd", [("so", 16)], lambda p, i=i: p.dma_start(
                        out=dbg_avf[:, i * 4096:(i + 1) * 4096], in_=avf[:, i * 4096:(i + 1) * 4096]))
                EM.do("gpsimd", [("so", 16)], lambda p: p.dma_start(out=dbg_q, in_=qsb[:, 1024:2048]))
                EM.do("gpsimd", [("so", 16)], lambda p: p.dma_start(out=dbg_k, in_=ksb[:, 1536:3072]))
                EM.do("gpsimd", [("so", 16)], lambda p: p.dma_start(out=dbg_p, in_=psb[:, 640:1280]))
                EM.do("gpsimd", [("so", 16)], lambda p: p.dma_start(out=dbg_pt, in_=ptsb[:, 640:1280]))
                for i in range(5):
                    EM.do("gpsimd", [("so", 16)], lambda p, i=i: p.dma_start(
                        out=dbg_v[:, i * 2048:(i + 1) * 2048], in_=vslab[:, (7 + i) * 2048:(8 + i) * 2048]))
            EM.barrier()
            attn_a_slot(1, xb)
            EM.barrier()
            if STAGE == 2:
                for i in range(8):
                    EM.do("gpsimd", [("so", 16)], lambda p, i=i: p.dma_start(
                        out=dbg_x1[i * 256:(i + 1) * 256, :], in_=x1T[i * 256:(i + 1) * 256, :]))
            if STAGE >= 2:
                ffn_block(x1T, x2T, first_pass=True, write_out=(STAGE == 2))
                EM.barrier()
            if STAGE >= 3:
                phase3(write_out=(STAGE == 3))
            if STAGE >= 4:
                ffn_block(x3T, None, first_pass=False, write_out=True)
                EM.barrier()
                if STAGE == 2:
                    for i in range(8):
                        EM.do("gpsimd", [("so", 16)], lambda p, i=i: p.dma_start(
                            out=dbg_h2[:, i * 4096:(i + 1) * 4096], in_=h2slab[:, i * 4096:(i + 1) * 4096]))
                    EM.barrier()

        with nc.Block() as block:
            @block.sync
            def _(p):
                program(Em("sync", p, sems, sems_cc))
            @block.gpsimd
            def _(p):
                program(Em("gpsimd", p, sems, sems_cc))
            @block.vector
            def _(p):
                program(Em("vector", p, sems, sems_cc))
            @block.scalar
            def _(p):
                program(Em("scalar", p, sems, sems_cc))
            @block.tensor
            def _(p):
                program(Em("tensor", p, sems, sems_cc))

    return nc


def _prep_core(c, I):
    x0 = I["L0_attn_a_x"]
    j0, j1 = c, 7 - c
    def xslice(b, j):
        lo = j * BLK - 512
        blkx = x0[b, max(lo, 0): j * BLK + BLK].T.astype(np.float32)
        if lo < 0:
            blkx = np.concatenate([np.zeros((E, -lo), np.float32), blkx], axis=1)
        return np.ascontiguousarray(blkx)
    wqkv = I["L0_attn_a_w_qkv"]
    m = {}
    m["xa"] = xslice(0, j0)
    m["xb"] = xslice(1, j1)
    m["wq_a"] = np.ascontiguousarray(wqkv[0:2048].T)
    m["wk_a"] = np.ascontiguousarray(wqkv[2048:4096].T)
    m["wv_a"] = np.ascontiguousarray(wqkv[4096:6144].T)
    m["wout"] = np.ascontiguousarray(I["L0_attn_a_w_out"].T)
    msk = np.zeros((16, 128, 640), np.float32)
    r = np.arange(128)[:, None]
    x = np.arange(640)[None, :]
    for sl, j in ((0, j0), (1, j1)):
        for qt in range(8):
            q0 = j * BLK + qt * 128
            allowed = (x <= 512 + r) & (x >= r) & (x >= 512 - q0)
            msk[sl * 8 + qt] = np.where(allowed, 0.0, NEG).astype(np.float32)
    m["mska"] = msk
    m["identbf"] = np.eye(128, dtype=np.float32).astype(ml_dtypes.bfloat16)
    m["wg"] = np.ascontiguousarray(I["L0_ffn_wq_gate"].T)
    m["wu"] = np.ascontiguousarray(I["L0_ffn_wq_up"].T)
    m["wd"] = np.ascontiguousarray(I["L0_ffn_wq_down"].T)
    m["sg"] = np.ascontiguousarray(I["L0_ffn_s_gate"].reshape(44, 128).T)
    m["su"] = np.ascontiguousarray(I["L0_ffn_s_up"].reshape(44, 128).T)
    m["sd"] = np.ascontiguousarray(I["L0_ffn_s_down"].reshape(16, 128).T)
    Wq = I["L0_attn_b_Wq"]; Wk = I["L0_attn_b_Wk"]; Wv = I["L0_attn_b_Wv"]
    m["wqb"] = np.ascontiguousarray(Wq.transpose(1, 0, 2).reshape(E, E))
    m["wkb"] = np.ascontiguousarray(Wk.transpose(1, 0, 2).reshape(E, 512))
    m["wvb"] = np.ascontiguousarray(Wv.transpose(1, 0, 2).reshape(E, 512))
    m["bqb"] = np.ascontiguousarray(I["L0_attn_b_bq"].T)
    m["bkb"] = np.ascontiguousarray(I["L0_attn_b_bk"].T)
    m["bvb"] = np.ascontiguousarray(I["L0_attn_b_bv"].reshape(1, 512))
    rr = np.arange(128)[:, None]
    uu = np.arange(3072)[None, :]
    m["mgm"] = np.where(uu <= rr + 2048, 0.0, NEG).astype(np.float32)
    se = 0 if j0 % 2 == 0 else 1
    jE = j0 if se == 0 else j1
    so = 1 - se
    jO = j1 if se == 0 else j0
    rank = lambda b, j: j if b == 0 else 7 - j
    p = np.arange(128, dtype=np.int64)
    idx = np.zeros((968, 128), np.int64)
    R_ = p3_rows()
    def fill_unit(kind, sp, Tn, b, blocks):
        idx[R_[(kind, sp, "q")]] = Tn * 128 + p
        nk = 4 if kind == "E" else 8
        for i in range(nk):
            g = i if kind == "E" else i // 2
            half = 0 if kind == "E" else i % 2
            r = blocks[half]
            idx[R_[(kind, sp, "K", i)]] = 2 * (512 * r + 128 * g + p) + b
        nv = 8 if kind == "E" else 16
        for i in range(nv):
            half = 0 if kind == "E" else i // 8
            ch = i if kind == "E" else i % 8
            r = blocks[half]
            idx[R_[(kind, sp, "V", i)]] = 2048 * r + 1024 * b + 128 * ch + p
        idx[R_[(kind, sp, "st")]] = Tn * 128 + p
        if kind == "F":
            idx[R_[(kind, sp, "stg")]] = Tn * 128 + p
    for sp in range(8):
        fill_unit("E", sp, se * 8 + sp, se, [rank(se, jE)])
        fill_unit("O", sp, so * 8 + sp, so, [rank(so, jO - 1), rank(so, jO)])
    def kcount(Tn):
        slot = Tn // 8
        j = j0 if slot == 0 else j1
        return j // 2 if j % 2 == 0 else (j - 1) // 2
    u = 0
    for k in range(4):
        for Tn in range(16):
            if kcount(Tn) > k:
                b = Tn // 8
                fill_unit("F", u, Tn, b, [rank(b, 2 * k), rank(b, 2 * k + 1)])
                u += 1
    assert u == 24, u
    m["idxs"] = np.ascontiguousarray(idx.T.astype(np.int32))
    return m


_NC_CACHE = {}

def kernel(**inputs):
    if "nc" not in _NC_CACHE:
        _NC_CACHE["nc"] = build_program()
    nc = _NC_CACHE["nc"]
    in_maps = [_prep_core(c, inputs) for c in range(8)]
    res = run_bass_kernel_spmd(nc, in_maps, core_ids=list(range(8)))
    full = np.zeros((B, S, E), np.float32)
    for c in range(8):
        o = res.results[c]["out"]
        full[0, c * BLK:(c + 1) * BLK] = o[:, 0:BLK].T
        full[1, (7 - c) * BLK:(8 - c) * BLK] = o[:, BLK:2 * BLK].T
    return full
